# version 22
# baseline (speedup 1.0000x reference)
import numpy as np
import ml_dtypes
from contextlib import ExitStack
import concourse.bass as bass
import concourse.mybir as mybir
from concourse.bass_utils import run_bass_kernel_spmd

F32 = mybir.dt.float32
BF16 = mybir.dt.bfloat16
AX = mybir.AxisListType.X
ALU = mybir.AluOpType
ACT = mybir.ActivationFunctionType

D = 1024
NH = 16
DK = 64
RAD = 8
NEXP = 16
EDIM = 512
R_CORE = 4096
TILE = 1024
NT = 4
NB = 8
NKC = 8
SCALE = 1.0 / 0.56
NEG = -1e30
LNW = 1040
HOF = 8
MOF = 16
GROUPS = [[16, 17], [0, 1, 2, 3], [4, 5, 6, 7], [8, 9, 10, 11], [12, 13, 14, 15]]

ENGINES = ("tensor", "vector", "scalar", "gpsimd", "sync")



class Plan:
    __slots__ = ("eng", "fn", "want", "sem", "val", "waits", "is_dma", "stream", "dbg")

    def __init__(self, eng, fn, is_dma=False, stream=None, dbg=""):
        self.eng = eng
        self.fn = fn
        self.want = is_dma
        self.sem = None
        self.val = None
        self.waits = []
        self.is_dma = is_dma
        self.stream = stream
        self.dbg = dbg


class Buf:
    __slots__ = ("t", "w", "r", "name")

    def __init__(self, t, name=""):
        self.t = t
        self.w = None
        self.r = []
        self.name = name

    def __getitem__(self, idx):
        return self.t[idx]

    def ap(self):
        return self.t.ap()


def col_bufs(buf, n):
    return [Buf(buf.t, f"{buf.name}.c{i}") for i in range(n)]


class Bld:
    def __init__(self):
        self.streams = {e: [] for e in ENGINES}
        self.dma_cum = {}
        self.fence_deps = None
        self.fenced = set()
        self.n = 0

    def _dep_tok(self, dep):
        if isinstance(dep, Plan):
            if dep.is_dma:
                return ("D" + dep.stream, self.dma_cum[dep.stream])
            dep.want = True
            return dep
        return dep

    def op(self, eng, fn, reads=(), writes=(), dbg=""):
        p = Plan(eng, fn, dbg=dbg)
        deps = []
        if self.fence_deps is not None and eng not in self.fenced:
            deps.extend(self.fence_deps)
            self.fenced.add(eng)
        for b in reads:
            if b.w is not None:
                deps.append(b.w)
        for b in writes:
            if b.w is not None:
                deps.append(b.w)
            deps.extend(b.r)
        if eng == "tensor":
            deps = [d for d in deps
                    if not (isinstance(d, Plan) and not d.is_dma and d.eng == "tensor")]
        p.waits = [self._dep_tok(d) for d in deps]
        self.streams[eng].append(p)
        for b in reads:
            b.r.append(p)
        for b in writes:
            b.w = p
            b.r = []
        self.n += 1
        return p

    def dma(self, eng, stream, fn, reads=(), writes=(), dbg=""):
        p = Plan(eng, fn, is_dma=True, stream=stream, dbg=dbg)
        deps = []
        if self.fence_deps is not None and eng not in self.fenced:
            deps.extend(self.fence_deps)
            self.fenced.add(eng)
        for b in reads:
            if b.w is not None:
                deps.append(b.w)
        for b in writes:
            if b.w is not None:
                deps.append(b.w)
            deps.extend(b.r)
        p.waits = [self._dep_tok(d) for d in deps]
        prev = self.dma_cum.get(stream, 0)
        if prev:
            p.waits.append(("D" + stream, prev))
        self.streams[eng].append(p)
        cum = prev + 16
        self.dma_cum[stream] = cum
        p.sem = "D" + stream
        p.val = cum
        for b in reads:
            b.r.append(p)
        for b in writes:
            b.w = p
            b.r = []
        self.n += 1
        return p

    def fence(self):
        deps = []
        for e in ENGINES:
            st = self.streams[e]
            for p in reversed(st):
                if not p.is_dma:
                    p.want = True
                    deps.append(p)
                    break
        for sname, cum in self.dma_cum.items():
            deps.append(("D" + sname, cum))
        self.fence_deps = deps
        self.fenced = set()

    def finalize_emit(self, nc, block, es):
        sem_names = set()
        for e in ENGINES:
            cum = 0
            for p in self.streams[e]:
                if p.is_dma:
                    sem_names.add(p.sem)
                elif p.want:
                    cum += 1
                    p.sem = "E" + e
                    p.val = cum
                    sem_names.add(p.sem)
        for p in (q for e in ENGINES for q in self.streams[e]):
            rw = []
            for w in p.waits:
                if isinstance(w, Plan):
                    assert w.sem is not None, f"dep without sem: {w.dbg}"
                    rw.append((w.sem, w.val))
                else:
                    rw.append(w)
            p.waits = rw
        handles = {s: es.enter_context(nc.semaphore(s)) for s in sorted(sem_names)}

        def emit(engine_attr, ename):
            def body(eng):
                mark = {}
                for p in self.streams[ename]:
                    need = {}
                    for s, v in p.waits:
                        if v > need.get(s, -1):
                            need[s] = v
                    for s, v in sorted(need.items()):
                        if mark.get(s, -1) < v:
                            eng.wait_ge(handles[s], v)
                            mark[s] = v
                    instr = p.fn(eng)
                    if p.sem is not None:
                        instr.then_inc(handles[p.sem], 16 if p.is_dma else 1)
            engine_attr(body)

        emit(block.tensor, "tensor")
        emit(block.vector, "vector")
        emit(block.scalar, "scalar")
        emit(block.gpsimd, "gpsimd")
        emit(block.sync, "sync")



def build_program(phase_a_only=False, n_tiles=NT, debug_taps=False, dbg_stage=99):
    nc = bass.Bass("TRN2", target_bir_lowering=False)
    B = Bld()

    dram = {}
    for name, shape, dt in [
        ("x", [R_CORE + 8, D], F32),
        ("wq", [D, D], F32), ("wk", [D, D], F32), ("wv", [D, D], F32), ("wo", [D, D], F32),
        ("rw", [NEXP, D], F32), ("rb", [NEXP], F32),
        ("upw", [NEXP, D, EDIM], F32), ("gw", [NEXP, D, EDIM], F32),
        ("dw", [NEXP, EDIM, D], F32),
        ("shgT", [D, D], F32), ("shuT", [D, D], F32), ("shdT", [D, D], F32),
        ("ident", [128, 128], F32),
        ("sel", [32, NH * 128], BF16),
        ("bmask", [128, 136], F32), ("fmask", [128, 136], F32),
        ("mT1", [128, 128], F32), ("mT0", [8, 128], F32), ("mT0f", [8, 128], F32),
    ]:
        dram[name] = Buf(nc.dram_tensor(name, shape, dt, kind="ExternalInput").ap().tensor, name)
    out_h = Buf(nc.dram_tensor("out", [R_CORE, D], F32, kind="ExternalOutput").ap().tensor, "out")
    x1h = Buf(nc.dram_tensor("x1h", [R_CORE, D], F32).ap().tensor, "x1h")
    chT = Buf(nc.dram_tensor("chT", [NEXP, TILE], F32).ap().tensor, "chT")

    es = ExitStack()

    def sbuf(name, shape, dt):
        return Buf(es.enter_context(nc.sbuf_tensor(name, shape, dt)), name)

    def psum(name, shape, dt=F32):
        return Buf(es.enter_context(nc.psum_tensor(name, shape, dt)), name)

    block = es.enter_context(nc.Block())

    xblk = [sbuf(f"xblk{i}", [128, 1024], F32) for i in range(NB)]
    ln_tok = sbuf("ln_tok", [128, 4 * 1024], BF16)
    ln_slot = col_bufs(ln_tok, 4)
    ln_feat = sbuf("ln_feat", [128, NKC * LNW], BF16)
    ident = sbuf("ident_sb", [128, 128], F32)
    sel = sbuf("sel_sb", [32, NH * 128], BF16)
    rwT = sbuf("rwT", [128, NKC * 16], BF16)
    rb_bc = sbuf("rb_bc", [128, NEXP], F32)
    neg1 = sbuf("neg1", [128, 1], F32)
    ssum = sbuf("ssum", [128, NB], F32)
    ssq = sbuf("ssq", [128, NB], F32)
    mean_a = sbuf("mean_a", [128, NB], F32)
    var_a = sbuf("var_a", [128, NB], F32)
    rstd_a = sbuf("rstd_a", [128, NB], F32)
    msq = sbuf("msq", [128, NB], F32)
    ssum_h = col_bufs(ssum, 2)
    ssq_h = col_bufs(ssq, 2)
    mean_h = col_bufs(mean_a, 2)
    var_h = col_bufs(var_a, 2)
    rstd_h = col_bufs(rstd_a, 2)
    msq_h = col_bufs(msq, 2)

    PA = [psum(f"PA{i}", [128, 512], F32) for i in range(4)]
    PB = [psum(f"PB{i}", [128, 1024], F32) for i in range(2)]

    def t_op(fn, reads=(), writes=(), dbg=""):
        return B.op("tensor", fn, reads, writes, dbg)

    def v_op(fn, reads=(), writes=(), dbg=""):
        return B.op("vector", fn, reads, writes, dbg)

    def s_op(fn, reads=(), writes=(), dbg=""):
        return B.op("scalar", fn, reads, writes, dbg)

    def ydma(stream, fn, reads=(), writes=(), dbg=""):
        return B.dma("sync", stream, fn, reads, writes, dbg)

    def sdma(stream, fn, reads=(), writes=(), dbg=""):
        return B.dma("scalar", stream, fn, reads, writes, dbg)

    def gdma(stream, fn, reads=(), writes=(), dbg=""):
        return B.dma("gpsimd", stream, fn, reads, writes, dbg)

    xchain = Buf(None, "xchain")

    def tp_dma(eng_dma, stream, fn, reads=(), writes=(), dbg=""):
        return eng_dma(stream, fn, tuple(reads) + (xchain,), tuple(writes), dbg)

    def sb2sb_dma(eng_dma, stream, fn, reads=(), writes=(), dbg=""):
        return eng_dma(stream, fn, tuple(reads), tuple(writes) + (xchain,), dbg)

    ssum_c = None

    def ln_sums(blks, b, width, dst_tok):
        sb_ = blks[b]
        dsl = slice((b % 4) * width, (b % 4) * width + width)
        v_op(lambda e, x=sb_, b=b: e.reduce_sum(ssum[:, b:b + 1], x[:], axis=AX),
             [sb_], [ssum_h[b // 4]], "ln_sum")
        s_op(lambda e, x=sb_, d=dsl, b=b: e.activation(dst_tok[:, d], x[:],
                                                       ACT.Square, accum_out=ssq[:, b:b + 1]),
             [sb_], [ln_slot[b % 4], ssq_h[b // 4]], "ln_sq")

    def ln_stats_half(hf, width):
        c = slice(hf * 4, hf * 4 + 4)
        v_op(lambda e: e.tensor_scalar(mean_a[:, c], ssum[:, c], 1.0 / width, None, ALU.mult),
             [ssum_h[hf]], [mean_h[hf]], "ln_mean")
        v_op(lambda e: e.tensor_scalar(var_a[:, c], ssq[:, c], 1.0 / width, None, ALU.mult),
             [ssq_h[hf]], [var_h[hf]], "ln_var0")
        v_op(lambda e: e.tensor_tensor(msq[:, c], mean_a[:, c], mean_a[:, c], ALU.mult),
             [mean_h[hf]], [msq_h[hf]], "ln_msq")
        v_op(lambda e: e.tensor_tensor(var_a[:, c], var_a[:, c], msq[:, c], ALU.subtract),
             [var_h[hf], msq_h[hf]], [var_h[hf]], "ln_var1")
        v_op(lambda e: e.tensor_scalar(var_a[:, c], var_a[:, c], 1e-5, None, ALU.add),
             [var_h[hf]], [var_h[hf]], "ln_eps")
        s_op(lambda e: e.activation(rstd_a[:, c], var_a[:, c], ACT.Ln),
             [var_h[hf]], [rstd_h[hf]], "ln_ln")
        v_op(lambda e: e.tensor_scalar(rstd_a[:, c], rstd_a[:, c], -0.5, None, ALU.mult),
             [rstd_h[hf]], [rstd_h[hf]], "ln_half")
        s_op(lambda e: e.activation(rstd_a[:, c], rstd_a[:, c], ACT.Exp),
             [rstd_h[hf]], [rstd_h[hf]], "ln_exp")

    def ln_pass(blks, nblocks, width, dst_tok, stats_cols):
        for hf in range(2):
            for b in range(hf * 4, hf * 4 + 4):
                ln_sums(blks, b, width, dst_tok)
            ln_stats_half(hf, width)


    lf4 = ln_feat.ap().rearrange("p (c w) -> p c w", c=NKC)

    def ln_apply_tr(blks, b):
        sb_ = blks[b]
        dsl = slice((b % 4) * 1024, (b % 4) * 1024 + 1024)
        v_op(lambda e, x=sb_, d=dsl, b=b: e.tensor_scalar(
            ln_tok[:, d], x[:], mean_a[:, b:b + 1], rstd_a[:, b:b + 1],
            ALU.subtract, ALU.mult),
            [sb_, mean_h[b // 4], rstd_h[b // 4]], [ln_slot[b % 4]], "ln_apply")
        dst = lf4[:, :, MOF + 128 * b: MOF + 128 * b + 128]
        tp_dma(ydma, "lnT%d" % b,
               lambda e, d=dst, ds=dsl: e.dma_start(d, ln_tok[:, ds], transpose=True),
               [ln_slot[b % 4]], [ln_feat], "lnT")

    def ln_apply_transpose(blks, nblocks):
        for b in range(nblocks):
            ln_apply_tr(blks, b)

    a_es = ExitStack()

    def a_sbuf(name, shape, dt):
        return Buf(a_es.enter_context(nc.sbuf_tensor(name, shape, dt)), name)

    for nm, buf in [("ident", ident), ("sel", sel)]:
        gdma("wsh", lambda e, n=nm, bb=buf: e.dma_start(bb[:], dram[n][:]), [dram[nm]], [buf], "c_" + nm)
    B.op("vector", lambda e: e.memset(neg1[:], -1.0), [], [neg1], "neg1")
    masks = {}
    for nm, shape in [("bmask", [128, 136]), ("fmask", [128, 136]),
                      ("mT1", [128, 128]), ("mT0", [8, 128]), ("mT0f", [8, 128])]:
        masks[nm] = sbuf(nm + "_sb", shape, F32)
        gdma("wsh", lambda e, n=nm: e.dma_start(masks[n][:], dram[n][:]), [dram[nm]], [masks[nm]], "c_" + nm)
    rb_src = bass.AP(dram["rb"].t, 0, [[0, 128], [1, NEXP]])
    gdma("wsh", lambda e: e.dma_start(rb_bc[:], rb_src), [dram["rb"]], [rb_bc], "c_rb")
    rw_f32 = a_sbuf("rw_f32", [16, 1024], F32)
    gdma("wsh", lambda e: e.dma_start(rw_f32[:], dram["rw"][:]), [dram["rw"]], [rw_f32], "c_rw")
    for kc in range(NKC):
        t_op(lambda e, k=kc: e.matmul(PA[0][0:128, 0:16], rw_f32[0:16, k * 128:(k + 1) * 128],
                                      ident[0:16, 0:16], start=True, stop=True, is_transpose=True),
             [rw_f32, ident], [PA[0]], "rwT_t")
        v_op(lambda e, k=kc: e.tensor_copy(rwT[:, k * 16:(k + 1) * 16], PA[0][0:128, 0:16]),
             [PA[0]], [rwT], "rwT_c")

    x_halo = a_sbuf("x_halo", [8, 1024], F32)
    q_feat = a_sbuf("q_feat", [128, 8 * 1024], BF16)
    qf_c = col_bufs(q_feat, NKC)
    k_feat = a_sbuf("k_feat", [128, 8 * LNW], BF16)
    kf_c = col_bufs(k_feat, NKC)
    vT = a_sbuf("vT", [128, NB * 1024], BF16)
    vhalo_l = [a_sbuf("vhalo0", [8, 8 * 1024], BF16)]
    attn_flat = a_sbuf("attn_flat", [128, 8 * 1024], BF16)
    WA = [a_sbuf(f"WA{i}", [128, NKC * 1024], BF16) for i in range(2)]
    hsums = a_sbuf("hsums", [8, 8], F32)
    negm_l = [a_sbuf(f"negm{i}", [128, NH], F32) for i in range(8)]
    negmc_l = [col_bufs(negm_l[i], NH) for i in range(8)]
    rall_l = [a_sbuf(f"rall{i}", [128, NH], F32) for i in range(8)]
    rallc_l = [col_bufs(rall_l[i], NH) for i in range(8)]
    biasq_l = [a_sbuf(f"biasq{i}", [128, NH], F32) for i in range(8)]
    chib_l = [a_sbuf(f"chib{i}", [128, NH], BF16) for i in range(1)]
    pack_l = [a_sbuf(f"pack{i}", [128, 32], F32) for i in range(8)]
    ctb_l = [a_sbuf(f"ctb{i}", [32, 128], BF16) for i in range(8)]
    sm_l = [a_sbuf(f"sm{i}", [128, 136], F32) for i in range(NH)]
    wts1_l = [a_sbuf(f"wts1{i}", [128, 256], BF16) for i in range(4)]
    wts0_l = [a_sbuf(f"wts0{i}", [8, 256], BF16) for i in range(4)]

    kf4 = k_feat.ap().rearrange("p (c w) -> p c w", c=NKC)
    qf4 = q_feat.ap().rearrange("p (c w) -> p c w", c=NKC)

    vhalo = None

    def vh_ap(q, lo, hi):
        return vhalo[0:8, q * 1024 + lo: q * 1024 + hi]

    def load_wa(slot, name, dbg):
        gdma("wa%d" % slot, lambda e: e.dma_start(
            WA[slot].ap().rearrange("p (c w) -> p c w", c=NKC),
            dram[name].ap().rearrange("(c p) w -> p c w", p=128)),
            [dram[name]], [WA[slot]], dbg)

    load_wa(0, "wq", "wq0")
    load_wa(1, "wk", "wk0")

    def phase_a_tile(t):
        nonlocal vhalo
        vhalo = vhalo_l[0]
        t0 = t * TILE
        if t > 0:
            v_op(lambda e: e.tensor_copy(kf4[:, :, HOF:HOF + 8], kf4[:, :, MOF + 1016:MOF + 1024]),
                 list(kf_c), list(kf_c), "khalo_cp")
        for b in range(NB):
            ydma("dx%d" % b, lambda e, bb=b: e.dma_start(
                xblk[bb].ap().rearrange("p w -> p w"),
                dram["x"].ap()[8 + t0 + 128 * bb: 8 + t0 + 128 * (bb + 1), :]),
                [dram["x"]], [xblk[b]], f"x{t}_{b}")
        if t == 0:
            ydma("dx", lambda e: e.dma_start(x_halo[:], dram["x"].ap()[t0:t0 + 8, :]),
                 [dram["x"]], [x_halo], "xh")
        for hf in range(2):
            for b in range(hf * 4, hf * 4 + 4):
                ln_sums(xblk, b, 1024, ln_tok)
            ln_stats_half(hf, 1024)
            for b in range(hf * 4, hf * 4 + 4):
                ln_apply_tr(xblk, b)
        if t > 0:
            sb2sb_dma(ydma, "vh0", lambda e: e.dma_start(
                vh_ap(0, 0, 1024), vT[120:128, 7 * 1024:8 * 1024]),
                [vT], [vhalo], "vhalo0")
        if dbg_stage < 1:
            return
        if t == 0:
            v_op(lambda e: e.reduce_sum(hsums[:, 0:1], x_halo[:], axis=AX), [x_halo], [hsums], "hln")
            s_op(lambda e: e.activation(ln_tok[0:8, 0:1024], x_halo[:], ACT.Square,
                                        accum_out=hsums[:, 1:2]),
                 [x_halo], [ln_slot[0], hsums], "hsq")
            v_op(lambda e: e.tensor_scalar(hsums[:, 2:3], hsums[:, 0:1], 1.0 / 1024, None, ALU.mult),
                 [hsums], [hsums], "hm")
            v_op(lambda e: e.tensor_scalar(hsums[:, 3:4], hsums[:, 1:2], 1.0 / 1024, None, ALU.mult),
                 [hsums], [hsums], "hv")
            v_op(lambda e: e.tensor_tensor(hsums[:, 4:5], hsums[:, 2:3], hsums[:, 2:3], ALU.mult),
                 [hsums], [hsums], "hmsq")
            v_op(lambda e: e.tensor_tensor(hsums[:, 3:4], hsums[:, 3:4], hsums[:, 4:5], ALU.subtract),
                 [hsums], [hsums], "hv2")
            v_op(lambda e: e.tensor_scalar(hsums[:, 3:4], hsums[:, 3:4], 1e-5, None, ALU.add),
                 [hsums], [hsums], "heps")
            s_op(lambda e: e.activation(hsums[:, 5:6], hsums[:, 3:4], ACT.Ln), [hsums], [hsums], "hln2")
            v_op(lambda e: e.tensor_scalar(hsums[:, 5:6], hsums[:, 5:6], -0.5, None, ALU.mult),
                 [hsums], [hsums], "hhalf")
            s_op(lambda e: e.activation(hsums[:, 5:6], hsums[:, 5:6], ACT.Exp), [hsums], [hsums], "hexp")
            v_op(lambda e: e.tensor_scalar(x_halo[:], x_halo[:], hsums[:, 2:3], hsums[:, 5:6],
                                           ALU.subtract, ALU.mult),
                 [x_halo, hsums], [x_halo], "happly")
            for kc in range(NKC):
                t_op(lambda e, k=kc: e.matmul(PA[0][0:128, k * 8:(k + 1) * 8],
                                              x_halo[0:8, k * 128:(k + 1) * 128],
                                              ident[0:8, 0:8], start=True, stop=True,
                                              is_transpose=True),
                     [x_halo, ident], [PA[0]], "hT")
            v_op(lambda e: e.tensor_copy(
                lf4[:, :, HOF:HOF + 8],
                bass.AP(PA[0].t, 0, [[PA[0].ap().ap[0][0], 128], [8, 8], [1, 8]])),
                [PA[0]], [ln_feat], "hTc")

        for nh in range(2):
            for mc in range(NKC):
                ps = PA[(mc * 2 + nh) % 2]
                for kc in range(NKC):
                    t_op(lambda e, m=mc, n=nh, k=kc, p=ps: e.matmul(
                        p[:, 0:512],
                        WA[0][:, k * 1024 + m * 128: k * 1024 + (m + 1) * 128],
                        lf4[:, k, MOF + n * 512: MOF + (n + 1) * 512],
                        start=(k == 0), stop=(k == NKC - 1)),
                        [WA[0], ln_feat], [ps], "qmm")
                v_op(lambda e, m=mc, n=nh, p=ps: e.tensor_scalar(
                    qf4[:, m, n * 512:(n + 1) * 512], p[:, 0:512], SCALE, None, ALU.mult),
                    [ps], [qf_c[mc]], "qev")
        load_wa(0, "wv", f"wv{t}")
        for nh in range(2):
            for mc in range(NKC):
                ps = PA[2 + (mc * 2 + nh) % 2]
                for kc in range(NKC):
                    t_op(lambda e, m=mc, n=nh, k=kc, p=ps: e.matmul(
                        p[:, 0:512],
                        WA[1][:, k * 1024 + m * 128: k * 1024 + (m + 1) * 128],
                        lf4[:, k, MOF + n * 512: MOF + (n + 1) * 512],
                        start=(k == 0), stop=(k == NKC - 1)),
                        [WA[1], ln_feat], [ps], "kmm")
                v_op(lambda e, m=mc, n=nh, p=ps: e.tensor_copy(
                    kf4[:, m, MOF + n * 512: MOF + (n + 1) * 512], p[:, 0:512]),
                    [ps], [kf_c[mc]], "kev")
        if t == 0:
            for mc in range(NKC):
                for kc in range(NKC):
                    t_op(lambda e, m=mc, k=kc: e.matmul(
                        PA[0][:, m * 8:(m + 1) * 8],
                        WA[1][:, k * 1024 + m * 128: k * 1024 + (m + 1) * 128],
                        lf4[:, k, HOF:HOF + 8], start=(k == 0), stop=(k == NKC - 1)),
                        [WA[1], ln_feat], [PA[0]], "khmm")
            v_op(lambda e: e.tensor_copy(
                kf4[:, :, HOF:HOF + 8],
                bass.AP(PA[0].t, 0, [[PA[0].ap().ap[0][0], 128], [8, 8], [1, 8]])),
                [PA[0]], list(kf_c), "khev")
        load_wa(1, "wo", f"wo{t}")
        stats_q = []

        def drain_stat():
            if stats_q:
                stats_q.pop(0)()

        def pass1_waves(q):
            nonlocal stats_q
            qp = q
            negm, rall, biasq = negm_l[qp], rall_l[qp], biasq_l[qp]
            negmc, rallc = negmc_l[qp], rallc_l[qp]
            lnr = rall
            chib, pack, ctb = chib_l[0], pack_l[qp], ctb_l[qp]
            first = (t == 0 and q == 0)
            p1mask = masks["fmask"] if first else masks["bmask"]
            for h in range(NH):
                ch, off = h // 2, 64 * (h % 2)
                sps = PA[2 + h % 2]
                sm = sm_l[h]
                t_op(lambda e, c=ch, o=off, qq=q, p=sps: e.matmul(
                    p[:, 0:136],
                    qf4[o:o + 64, c, qq * 128:(qq + 1) * 128],
                    kf4[o:o + 64, c, HOF + qq * 128: HOF + qq * 128 + 136],
                    start=True, stop=True),
                    [qf_c[ch], kf_c[ch]], [sps], "p1mm")
                v_op(lambda e, p=sps, s=sm, mk=p1mask: e.tensor_tensor(
                    s[:], p[:, 0:136], mk[:], ALU.add), [sps, p1mask], [sm], "p1mask")
                if h % 3 == 0:
                    drain_stat()
            for h in range(NH):
                sm = sm_l[h]
                v_op(lambda e, s=sm, hh=h, n=negm: e.reduce_max(
                    n[:, hh:hh + 1], s[:], axis=AX, negate=True), [sm], [negmc[h]], "p1max")
            for h in range(NH):
                sm = sm_l[h]
                s_op(lambda e, s=sm, hh=h, n=negm, r=rall: e.activation(
                    s[:], s[:], ACT.Exp, bias=n[:, hh:hh + 1],
                    accum_out=r[:, hh:hh + 1]),
                    [sm, negmc[h]], [sm, rallc[h]], "p1exp")
            s_op(lambda e, r=rall, l=lnr: e.activation(l[:], r[:], ACT.Ln),
                 list(rallc), list(rallc), "lnr")

            def _mk(qq, negm=negm, lnr=lnr, biasq=biasq, chib=chib, pack=pack, ctb=ctb,
                    negmc=negmc, rallc=rallc):
                return [
                    lambda: v_op(lambda e: e.tensor_tensor(biasq[:], negm[:], lnr[:], ALU.subtract),
                                 list(negmc) + list(rallc), [biasq], "bias"),
                    lambda: v_op(lambda e: e.tensor_copy(chib[:], biasq[:]), [biasq], [chib], "chib"),
                    lambda: v_op(lambda e: e.tensor_copy(pack[:, 0:16], chib[:]), [chib], [pack], "chif"),
                    lambda: v_op(lambda e: e.tensor_tensor(pack[:, 16:32], biasq[:], pack[:, 0:16],
                                                           ALU.subtract), [biasq, pack], [pack], "clo"),
                ]
            assert not stats_q
            stats_q = _mk(q)

        def ct_emit(q):
            pack, ctb = pack_l[q], ctb_l[q]
            tps = PA[2 + q % 2]
            t_op(lambda e, pk=pack, p=tps: e.matmul(
                p[0:32, 0:128], pk[:], ident[:], start=True, stop=True, is_transpose=True),
                [pack, ident], [tps], "cT")
            v_op(lambda e, p=tps, c=ctb: e.tensor_copy(c[:], p[0:32, 0:128]), [tps], [ctb], "cTc")
        for b in range(NB):
            ps = PB[b % 2]
            for kc in range(NKC):
                for nh in range(2):
                    t_op(lambda e, bb=b, k=kc, n=nh, p=ps: e.matmul(
                        p[:, n * 512:(n + 1) * 512],
                        lf4[:, k, MOF + bb * 128: MOF + (bb + 1) * 128],
                        WA[0][:, k * 1024 + n * 512: k * 1024 + (n + 1) * 512],
                        start=(k == 0), stop=(k == NKC - 1)),
                        [ln_feat, WA[0]], [ps], "vmm")
            v_op(lambda e, bb=b, p=ps: e.tensor_copy(vT[:, bb * 1024:(bb + 1) * 1024], p[:]),
                 [ps], [vT], "vev")
        if t == 0:
            for kc in range(NKC):
                for nh in range(2):
                    t_op(lambda e, k=kc, n=nh: e.matmul(
                        PB[0][0:8, n * 512:(n + 1) * 512],
                        lf4[:, k, HOF:HOF + 8],
                        WA[0][:, k * 1024 + n * 512: k * 1024 + (n + 1) * 512],
                        start=(k == 0), stop=(k == NKC - 1)),
                        [ln_feat, WA[0]], [PB[0]], "vhmm")
            v_op(lambda e: e.tensor_copy(vh_ap(0, 0, 1024), PB[0][0:8, :]), [PB[0]], [vhalo], "vhev")
        if t + 1 < n_tiles:
            load_wa(0, "wq", f"wq{t + 1}")
        for q in range(1, 8):
            sb2sb_dma(ydma, "vh%d" % (1 + (q % 2)), lambda e, qq=q: e.dma_start(
                vh_ap(qq, 0, 1024), vT[120:128, (qq - 1) * 1024: qq * 1024]),
                [vT], [vhalo], f"vhalo{q}")

        if dbg_stage < 2:
            return
        def pass2_pipe(q):
            ctb = ctb_l[q]
            first = (t == 0 and q == 0)
            m0mask = masks["mT0f"] if first else masks["mT0"]

            def p2_mms(m, q=q, ctb=ctb):
                pp = PA[m % 2]
                for i in range(2):
                    h = 2 * m + i
                    ch, off = h // 2, 64 * (h % 2)
                    t_op(lambda e, c=ch, o=off, qq=q, p=pp, ii=i: e.matmul(
                        p[:, ii * 128:(ii + 1) * 128],
                        kf4[o:o + 64, c, MOF + qq * 128: MOF + qq * 128 + 128],
                        qf4[o:o + 64, c, qq * 128:(qq + 1) * 128],
                        start=True, stop=False),
                        [kf_c[ch], qf_c[ch]], [pp], "p2mm1")
                    t_op(lambda e, hh=h, p=pp, ii=i, c=ctb: e.matmul(
                        p[:, ii * 128:(ii + 1) * 128],
                        sel[:, hh * 128:(hh + 1) * 128],
                        c[:], start=False, stop=True),
                        [sel, ctb], [pp], "p2mmc1")
                    t_op(lambda e, c=ch, o=off, qq=q, p=pp, ii=i: e.matmul(
                        p[0:8, 256 + ii * 128: 256 + (ii + 1) * 128],
                        kf4[o:o + 64, c, HOF + qq * 128: HOF + qq * 128 + 8],
                        qf4[o:o + 64, c, qq * 128:(qq + 1) * 128],
                        start=True, stop=False),
                        [kf_c[ch], qf_c[ch]], [pp], "p2mm0")
                    t_op(lambda e, hh=h, p=pp, ii=i, c=ctb: e.matmul(
                        p[0:8, 256 + ii * 128: 256 + (ii + 1) * 128],
                        sel[:, hh * 128: hh * 128 + 8],
                        c[:], start=False, stop=True),
                        [sel, ctb], [pp], "p2mmc0")

            def p2_madds(m, m0mask=m0mask):
                pp = PA[m % 2]
                mT1b = bass.AP(masks["mT1"].t, 0, [[masks["mT1"].ap().ap[0][0], 128], [0, 2], [1, 128]])
                mT0b = bass.AP(m0mask.t, 0, [[m0mask.ap().ap[0][0], 8], [0, 2], [1, 128]])
                v_op(lambda e, p=pp, mb=mT1b: e.tensor_tensor(
                    p.ap()[:, 0:256].rearrange("p (r w) -> p r w", r=2),
                    p.ap()[:, 0:256].rearrange("p (r w) -> p r w", r=2), mb, ALU.add),
                    [pp, masks["mT1"]], [pp], "p2m1")
                v_op(lambda e, p=pp, mb=mT0b: e.tensor_tensor(
                    p.ap()[0:8, 256:512].rearrange("p (r w) -> p r w", r=2),
                    p.ap()[0:8, 256:512].rearrange("p (r w) -> p r w", r=2), mb, ALU.add),
                    [pp, m0mask], [pp], "p2m0")

            def p2_exps(m):
                pp = PA[m % 2]
                wts1, wts0 = wts1_l[m % 4], wts0_l[m % 4]
                s_op(lambda e, p=pp, w=wts1: e.activation(w[:], p[:, 0:256], ACT.Exp),
                     [pp], [wts1], "p2exp1")
                s_op(lambda e, p=pp, w=wts0: e.activation(w[:], p[0:8, 256:512], ACT.Exp),
                     [pp], [wts0], "p2exp0")

            def p2_av(m, q=q):
                wts1, wts0 = wts1_l[m % 4], wts0_l[m % 4]
                avp = PB[m % 2]
                for i in range(2):
                    h = 2 * m + i
                    off = 64 * (h % 2)
                    tp = None if off == 0 else (0, 64)
                    t_op(lambda e, hh=h, o=off, qq=q, p=avp, w=wts1, ii=i, tpos=tp: e.matmul(
                        p[o:o + 64, 0:128],
                        vT[:, qq * 1024 + hh * 64: qq * 1024 + (hh + 1) * 64],
                        w[:, ii * 128:(ii + 1) * 128],
                        start=True, stop=False, tile_position=tpos),
                        [vT, wts1], [avp], "avmm1")
                    t_op(lambda e, hh=h, o=off, qq=q, p=avp, w=wts0, ii=i, tpos=tp: e.matmul(
                        p[o:o + 64, 0:128],
                        vh_ap(qq, hh * 64, (hh + 1) * 64),
                        w[:, ii * 128:(ii + 1) * 128],
                        start=False, stop=True, tile_position=tpos),
                        [vhalo, wts0], [avp], "avmm0")

            def p2_avev(m, q=q):
                avp = PB[m % 2]
                v_op(lambda e, p=avp, mm=m, qq=q: e.tensor_copy(
                    attn_flat[:, mm * 1024 + qq * 128: mm * 1024 + qq * 128 + 128],
                    p[:, 0:128]), [avp], [attn_flat], "avev")

            for step in range(NH // 2 + 3):
                if step < 8:
                    p2_mms(step)
                    p2_madds(step)
                if 1 <= step < 9:
                    p2_exps(step - 1)
                if 2 <= step < 10:
                    p2_av(step - 2)
                if 3 <= step:
                    p2_avev(step - 3)
            b = q
            af4 = attn_flat.ap().rearrange("p (c w) -> p c w", c=NKC)
            ps = PB[(b + 1) % 2]
            for kc in range(NKC):
                for nh in range(2):
                    t_op(lambda e, bb=b, k=kc, n=nh, p=ps: e.matmul(
                        p[:, n * 512:(n + 1) * 512],
                        af4[:, k, bb * 128:(bb + 1) * 128],
                        WA[1][:, k * 1024 + n * 512: k * 1024 + (n + 1) * 512],
                        start=(k == 0), stop=(k == NKC - 1)),
                        [attn_flat, WA[1]], [ps], "womm")
            v_op(lambda e, bb=b, p=ps: e.tensor_tensor(
                xblk[bb][:], p[:], xblk[bb][:], ALU.add),
                [ps, xblk[b]], [xblk[b]], "woev")
            sdma("x1w%d" % b, lambda e, bb=b: e.dma_start(
                x1h.ap()[t0 + 128 * bb: t0 + 128 * (bb + 1), :], xblk[bb].ap()),
                [xblk[b]], [x1h], f"x1w{t}_{b}")
            if phase_a_only:
                sdma("dout%d" % b, lambda e, bb=b: e.dma_start(
                    out_h.ap()[t0 + 128 * bb: t0 + 128 * (bb + 1), :], xblk[bb].ap()),
                    [xblk[b]], [out_h], f"outA{t}_{b}")

        pass1_waves(0)
        for q in range(NB):
            if q + 1 < NB:
                pass1_waves(q + 1)
            else:
                while stats_q:
                    drain_stat()
            ct_emit(q)
            pass2_pipe(q)
        if t + 1 < n_tiles:
            load_wa(1, "wk", f"wk{t + 1}")

    for t in range(n_tiles):
        phase_a_tile(t)
        if debug_taps and t == 0:
            for nm, buf, shape in [("d_lnf", ln_feat, [128, NKC * LNW]),
                                   ("d_lnt", ln_tok, [128, 4 * 1024]),
                                   ("d_q", q_feat, [128, 8 * 1024]),
                                   ("d_k", k_feat, [128, 8 * LNW]),
                                   ("d_vT", vT, [128, NB * 1024]),
                                   ("d_vh", vhalo, [8, 8 * 1024]),
                                   ("d_af", attn_flat, [128, 8 * 1024])]:
                dh = Buf(nc.dram_tensor(nm, shape, F32, kind="ExternalOutput").ap().tensor, nm)
                gdma("dbg", lambda e, dd=dh, bb=buf: e.dma_start(dd.ap(), bb.ap()),
                     [buf], [dh], "tap_" + nm)

    if phase_a_only:
        B.op("vector", lambda e: e.engine_nop(), [out_h], [], "fin")
        B.finalize_emit(nc, block, es)
        a_es.close()
        es.close()
        nc.finalize()
        return nc

    a_es.close()
    B.fence()

    b_es = ExitStack()

    def b_sbuf(name, shape, dt):
        return Buf(b_es.enter_context(nc.sbuf_tensor(name, shape, dt)), name)

    lg_sb = b_sbuf("lg_sb", [128, NB * 16], F32)
    c_sb = b_sbuf("c_sb", [128, NB * 16], F32)
    m1 = b_sbuf("m1", [128, NB], F32)
    m2 = b_sbuf("m2", [128, NB], F32)
    p1 = b_sbuf("p1", [128, NB], F32)
    p2 = b_sbuf("p2", [128, NB], F32)
    dif = b_sbuf("dif", [128, NB], F32)
    eq1 = b_sbuf("eq1", [128, 16], F32)
    eq2 = b_sbuf("eq2", [128, 16], F32)
    lgm = b_sbuf("lgm", [128, 16], F32)
    cT_sb = b_sbuf("cT_sb", [16, NB * 128], F32)
    c_ball = b_sbuf("c_ball", [128, 8 * 1024], BF16)
    h_g = b_sbuf("h_g", [128, 4 * 4 * 1024], BF16)
    hg_slot = col_bufs(h_g, 4)
    u_sb = b_sbuf("u_sb", [128, 4 * 512], BF16)
    gs_sb = b_sbuf("gs_sb", [128, 4 * 512], BF16)
    UPG = [b_sbuf(f"UPG{i}", [128, NKC * 1024], BF16) for i in range(2)]
    DWN = [b_sbuf(f"DWN{i}", [128, 4 * 1024], BF16) for i in range(4)]

    def load_upg(slot, e_idx, dbg):
        if e_idx < NEXP:
            up_src = dram["upw"].ap()[e_idx].rearrange("(c p) w -> p c w", p=128)
            g_src = dram["gw"].ap()[e_idx].rearrange("(c p) w -> p c w", p=128)
        else:
            k = e_idx - NEXP
            up_src = dram["shuT"].ap().rearrange("(c p) w -> p c w", p=128)[:, :, k * 512:(k + 1) * 512]
            g_src = dram["shgT"].ap().rearrange("(c p) w -> p c w", p=128)[:, :, k * 512:(k + 1) * 512]
        dst = UPG[slot].ap().rearrange("p (c t w) -> p c t w", c=NKC, t=2)
        rd = [dram["upw"], dram["gw"]] if e_idx < NEXP else [dram["shuT"], dram["shgT"]]
        gdma("wu%d" % slot, lambda e: e.dma_start(dst[:, :, 0, :], up_src), rd[:1], [UPG[slot]], dbg + "u")
        gdma("wu%d" % slot, lambda e: e.dma_start(dst[:, :, 1, :], g_src), rd[1:], [UPG[slot]], dbg + "g")

    def load_dwn(slot, e_idx, dbg):
        if e_idx < NEXP:
            src = dram["dw"].ap()[e_idx].rearrange("(c p) w -> p c w", p=128)
            rd = [dram["dw"]]
        else:
            k = e_idx - NEXP
            src = dram["shdT"].ap()[512 * k: 512 * (k + 1), :].rearrange("(c p) w -> p c w", p=128)
            rd = [dram["shdT"]]
        gdma("wd%d" % slot, lambda e: e.dma_start(
            DWN[slot].ap().rearrange("p (c w) -> p c w", c=4), src), rd, [DWN[slot]], dbg)

    b_head_done = set()

    def b_head(t, b):
        t0 = t * TILE
        ydma("x1r%d" % b, lambda e, bb=b: e.dma_start(
            xblk[bb].ap(), x1h.ap()[t0 + 128 * bb: t0 + 128 * (bb + 1), :]),
            [x1h], [xblk[b]], f"x1r{t}_{b}")
        b_head_done.add((t, b))

    def phase_b_tile(t):
        t0 = t * TILE
        eseq0 = [e for g in GROUPS for e in g]
        load_upg(0, eseq0[0], "e0")
        load_upg(1, eseq0[1], "e1")
        for sl, e_idx in enumerate(GROUPS[0]):
            load_dwn(sl, e_idx, f"d{e_idx}")
        for hf in range(2):
            for b in range(hf * 4, hf * 4 + 4):
                if (t, b) not in b_head_done:
                    b_head(t, b)
                ln_sums(xblk, b, 1024, ln_tok)
            ln_stats_half(hf, 1024)
            for b in range(hf * 4, hf * 4 + 4):
                ln_apply_tr(xblk, b)
        for b in range(NB):
            ps = PA[b % 2]
            for kc in range(NKC):
                t_op(lambda e, bb=b, k=kc, p=ps: e.matmul(
                    p[:, 0:16],
                    lf4[:, k, MOF + bb * 128: MOF + (bb + 1) * 128],
                    rwT[:, k * 16:(k + 1) * 16],
                    start=(k == 0), stop=(k == NKC - 1)),
                    [ln_feat, rwT], [ps], "rtmm")
            v_op(lambda e, bb=b, p=ps: e.tensor_tensor(
                lg_sb[:, bb * 16:(bb + 1) * 16], p[:, 0:16], rb_bc[:], ALU.add),
                [ps, rb_bc], [lg_sb], "rtev")
            v_op(lambda e, bb=b: e.reduce_max(m1[:, bb:bb + 1], lg_sb[:, bb * 16:(bb + 1) * 16], axis=AX),
                 [lg_sb], [m1], "t2m1")
            v_op(lambda e, bb=b: e.tensor_scalar(eq1[:], lg_sb[:, bb * 16:(bb + 1) * 16],
                                                 m1[:, bb:bb + 1], None, ALU.is_equal),
                 [lg_sb, m1], [eq1], "t2eq1")
            v_op(lambda e, bb=b: e.scalar_tensor_tensor(lgm[:], eq1[:], NEG,
                                                        lg_sb[:, bb * 16:(bb + 1) * 16],
                                                        ALU.mult, ALU.add),
                 [eq1, lg_sb], [lgm], "t2lgm")
            v_op(lambda e, bb=b: e.reduce_max(m2[:, bb:bb + 1], lgm[:], axis=AX), [lgm], [m2], "t2m2")
            v_op(lambda e, bb=b: e.tensor_scalar(eq2[:], lgm[:], m2[:, bb:bb + 1], None, ALU.is_equal),
                 [lgm, m2], [eq2], "t2eq2")
            v_op(lambda e, bb=b: e.tensor_tensor(dif[:, bb:bb + 1], m1[:, bb:bb + 1],
                                                 m2[:, bb:bb + 1], ALU.subtract),
                 [m1, m2], [dif], "t2dif")
            s_op(lambda e, bb=b: e.activation(p1[:, bb:bb + 1], dif[:, bb:bb + 1], ACT.Sigmoid),
                 [dif], [p1], "t2p1")
            v_op(lambda e, bb=b: e.tensor_scalar(p2[:, bb:bb + 1], p1[:, bb:bb + 1], -1.0, 1.0,
                                                 ALU.mult, ALU.add), [p1], [p2], "t2p2")
            v_op(lambda e, bb=b: e.tensor_scalar(c_sb[:, bb * 16:(bb + 1) * 16], eq1[:],
                                                 p1[:, bb:bb + 1], None, ALU.mult),
                 [eq1, p1], [c_sb], "t2c1")
            v_op(lambda e, bb=b: e.scalar_tensor_tensor(
                c_sb[:, bb * 16:(bb + 1) * 16], eq2[:], p2[:, bb:bb + 1],
                c_sb[:, bb * 16:(bb + 1) * 16], ALU.mult, ALU.add),
                [eq2, p2, c_sb], [c_sb], "t2c2")
            tps = PA[2 + b % 2]
            t_op(lambda e, bb=b, p=tps: e.matmul(
                p[0:16, 0:128], c_sb[:, bb * 16:(bb + 1) * 16], ident[:],
                start=True, stop=True, is_transpose=True),
                [c_sb, ident], [tps], "cTmm")
            v_op(lambda e, bb=b, p=tps: e.tensor_copy(cT_sb[:, bb * 128:(bb + 1) * 128],
                                                      p[0:16, 0:128]), [tps], [cT_sb], "cTev")
        sdma("cw", lambda e: e.dma_start(chT.ap(), cT_sb.ap()), [cT_sb], [chT], f"cw{t}")
        cb_src0 = bass.AP(chT.t, 0, [[0, 128], [1, 8 * TILE]])
        gdma("cb", lambda e: e.dma_start(c_ball[:], cb_src0), [chT], [c_ball], f"cb{t}a")

        eseq = [e for g in GROUPS for e in g]
        for g_i, group in enumerate(GROUPS):
            if group and group[0] == 8:
                cb_src1 = bass.AP(chT.t, 8 * TILE, [[0, 128], [1, 8 * TILE]])
                gdma("cb", lambda e: e.dma_start(c_ball[:], cb_src1), [chT], [c_ball], f"cb{t}b")
            for ei, e_idx in enumerate(group):
                slot = eseq.index(e_idx) % 2
                shared = e_idx >= NEXP
                for s in range(2):
                    for mc in range(4):
                        pu = PA[mc]
                        for kc in range(NKC):
                            t_op(lambda e, m=mc, k=kc, ss=s, sl=slot, p=pu: e.matmul(
                                p[:, 0:512],
                                UPG[sl][:, k * 1024 + m * 128:k * 1024 + (m + 1) * 128],
                                lf4[:, k, MOF + ss * 512: MOF + (ss + 1) * 512],
                                start=(k == 0), stop=(k == NKC - 1)),
                                [UPG[slot], ln_feat], [pu], "umm")
                        if shared:
                            v_op(lambda e, m=mc, p=pu: e.tensor_copy(
                                u_sb[:, m * 512:(m + 1) * 512], p[:, 0:512]),
                                [pu], [u_sb], "uev_sh")
                        else:
                            v_op(lambda e, m=mc, ss=s, ee=e_idx % 8, p=pu: e.tensor_tensor(
                                u_sb[:, m * 512:(m + 1) * 512], p[:, 0:512],
                                c_ball[:, ee * 1024 + ss * 512: ee * 1024 + (ss + 1) * 512],
                                ALU.mult), [pu, c_ball], [u_sb], "uev")
                    for mg in range(2):
                        pg = PB[mg]
                        for mi in range(2):
                            mc = mg * 2 + mi
                            for kc in range(NKC):
                                t_op(lambda e, m=mc, k=kc, ss=s, sl=slot, p=pg, mm=mi: e.matmul(
                                    p[:, mm * 512:(mm + 1) * 512],
                                    UPG[sl][:, k * 1024 + 512 + m * 128: k * 1024 + 512 + (m + 1) * 128],
                                    lf4[:, k, MOF + ss * 512: MOF + (ss + 1) * 512],
                                    start=(k == 0), stop=(k == NKC - 1)),
                                    [UPG[slot], ln_feat], [pg], "gmm")
                        s_op(lambda e, g=mg, p=pg: e.activation(
                            gs_sb[:, g * 1024:(g + 1) * 1024], p[:], ACT.Silu),
                            [pg], [gs_sb], "silu")
                    hdst = h_g.ap().rearrange("p (e m w) -> p e m w", e=4, m=4)[:, ei, :, s * 512:(s + 1) * 512]
                    v_op(lambda e, d=hdst: e.tensor_tensor(
                        d, u_sb.ap().rearrange("p (m w) -> p m w", m=4),
                        gs_sb.ap().rearrange("p (m w) -> p m w", m=4), ALU.mult),
                        [u_sb, gs_sb], [hg_slot[ei]], "hmul")
                pos = eseq.index(e_idx)
                if pos + 2 < len(eseq):
                    load_upg(slot, eseq[pos + 2], f"e{eseq[pos + 2]}")
            for b in range(NB):
                pd = PB[b % 2]
                nlast = len(group) - 1
                for ei, e_idx in enumerate(group):
                    dslot = ei
                    for hc in range(4):
                        for nh in range(2):
                            first_mm = (ei == 0 and hc == 0)
                            last_mm = (ei == nlast and hc == 3)
                            t_op(lambda e, eii=ei, h=hc, n=nh, bb=b, p=pd, sl=dslot,
                                 st=first_mm, sp=last_mm: e.matmul(
                                     p[:, n * 512:(n + 1) * 512],
                                     h_g.ap().rearrange("p (x m w) -> p x m w", x=4, m=4)[:, eii, h, bb * 128:(bb + 1) * 128],
                                     DWN[sl][:, h * 1024 + n * 512: h * 1024 + (n + 1) * 512],
                                     start=st, stop=sp),
                                [hg_slot[ei], DWN[dslot]], [pd], "dmm")
                v_op(lambda e, bb=b, p=pd: e.tensor_tensor(
                    xblk[bb][:], p[:], xblk[bb][:], ALU.add),
                    [pd, xblk[b]], [xblk[b]], "dacc")
                if g_i + 1 == len(GROUPS):
                    sdma("dout%d" % b, lambda e, bb=b: e.dma_start(
                        out_h.ap()[t0 + 128 * bb: t0 + 128 * (bb + 1), :], xblk[bb].ap()),
                        [xblk[b]], [out_h], f"out{t}_{b}")
                    if t + 1 < n_tiles:
                        b_head(t + 1, b)
            if g_i + 1 < len(GROUPS):
                for ei, e_idx in enumerate(GROUPS[g_i + 1]):
                    load_dwn(ei, e_idx, f"d{e_idx}")


    for t in range(n_tiles):
        phase_b_tile(t)

    B.op("vector", lambda e: e.engine_nop(), [out_h], [], "fin")
    B.finalize_emit(nc, block, es)
    b_es.close()
    es.close()
    nc.finalize()
    return nc



def _make_consts():
    i_idx = np.arange(128)[:, None]
    j_idx = np.arange(136)[None, :]
    band = np.where((i_idx <= j_idx) & (j_idx <= i_idx + RAD), 0.0, NEG).astype(np.float32)
    fmask_even = band.copy()
    fmask_even[:, 0:8] = NEG
    jj = np.arange(128)[:, None]
    tt = np.arange(128)[None, :]
    mT1 = np.where((jj <= tt) & (tt <= jj + RAD), 0.0, NEG).astype(np.float32)
    j0 = np.arange(8)[:, None]
    mT0 = np.where(tt <= j0, 0.0, NEG).astype(np.float32)
    ident = np.eye(128, dtype=np.float32)
    sel = np.zeros((32, NH * 128), np.float32)
    for h in range(NH):
        sel[h, h * 128:(h + 1) * 128] = 1.0
        sel[NH + h, h * 128:(h + 1) * 128] = 1.0
    bf = ml_dtypes.bfloat16
    return {
        "band": band, "fmask_even": fmask_even,
        "mT1": mT1, "mT0": mT0,
        "mT0z": np.full_like(mT0, NEG),
        "ident": ident, "sel": sel.astype(bf),
    }


def kernel(L0_attn_x, L0_attn_Wq, L0_attn_Wk, L0_attn_Wv, L0_attn_Wo,
           L0_ffn_router_w, L0_ffn_router_bias, L0_ffn_up_proj, L0_ffn_gate_proj,
           L0_ffn_down_proj, L0_ffn_sh_gate, L0_ffn_sh_up, L0_ffn_sh_down):
    x = np.ascontiguousarray(np.asarray(L0_attn_x, dtype=np.float32))
    Bsz, S, _ = x.shape
    consts = _make_consts()
    common = {
        "wq": np.ascontiguousarray(L0_attn_Wq, np.float32),
        "wk": np.ascontiguousarray(L0_attn_Wk, np.float32),
        "wv": np.ascontiguousarray(L0_attn_Wv, np.float32),
        "wo": np.ascontiguousarray(L0_attn_Wo, np.float32),
        "rw": np.ascontiguousarray(L0_ffn_router_w, np.float32),
        "rb": np.ascontiguousarray(L0_ffn_router_bias, np.float32),
        "upw": np.ascontiguousarray(L0_ffn_up_proj, np.float32),
        "gw": np.ascontiguousarray(L0_ffn_gate_proj, np.float32),
        "dw": np.ascontiguousarray(L0_ffn_down_proj, np.float32),
        "shgT": np.ascontiguousarray(np.asarray(L0_ffn_sh_gate, np.float32).T),
        "shuT": np.ascontiguousarray(np.asarray(L0_ffn_sh_up, np.float32).T),
        "shdT": np.ascontiguousarray(np.asarray(L0_ffn_sh_down, np.float32).T),
        "ident": consts["ident"],
        "sel": consts["sel"],
        "bmask": consts["band"],
        "mT1": consts["mT1"],
        "mT0": consts["mT0"],
    }
    in_maps = []
    for c in range(8):
        b, half = c // 2, c % 2
        rows = x[b, half * R_CORE:(half + 1) * R_CORE, :]
        if half == 0:
            halo = np.zeros((8, D), np.float32)
            fmask = consts["fmask_even"]
            mT0f = consts["mT0z"]
        else:
            halo = x[b, half * R_CORE - 8: half * R_CORE, :]
            fmask = consts["band"]
            mT0f = consts["mT0"]
        m = dict(common)
        m["x"] = np.ascontiguousarray(np.concatenate([halo, rows], axis=0))
        m["fmask"] = fmask
        m["mT0f"] = mT0f
        in_maps.append(m)

    nc = build_program()
    res = run_bass_kernel_spmd(nc, in_maps, core_ids=list(range(8)))
    shards = [res.results[c]["out"] for c in range(8)]
    full = np.concatenate(shards, axis=0).reshape(Bsz, S, D)
    return full.astype(np.float32)


if __name__ == "__main__":
    nc = build_program()
    print("built ok")


# version 23
# speedup vs baseline: 1.2193x; 1.2193x over previous
import numpy as np
import ml_dtypes
from contextlib import ExitStack
import concourse.bass as bass
import concourse.mybir as mybir
from concourse.bass_utils import run_bass_kernel_spmd

F32 = mybir.dt.float32
BF16 = mybir.dt.bfloat16
AX = mybir.AxisListType.X
ALU = mybir.AluOpType
ACT = mybir.ActivationFunctionType

D = 1024
NH = 16
DK = 64
RAD = 8
NEXP = 16
EDIM = 512
R_CORE = 4096
TILE = 1024
NT = 4
NB = 8
NKC = 8
SCALE = 1.0 / 0.56
NEG = -1e30
LNW = 1040
HOF = 8
MOF = 16
GROUPS = [[16, 17], [0, 1, 2, 3], [4, 5, 6, 7], [8, 9, 10, 11], [12, 13, 14, 15]]

ENGINES = ("tensor", "vector", "scalar", "gpsimd", "sync")



class Plan:
    __slots__ = ("eng", "fn", "want", "sem", "val", "waits", "is_dma", "stream", "dbg")

    def __init__(self, eng, fn, is_dma=False, stream=None, dbg=""):
        self.eng = eng
        self.fn = fn
        self.want = is_dma
        self.sem = None
        self.val = None
        self.waits = []
        self.is_dma = is_dma
        self.stream = stream
        self.dbg = dbg


class Buf:
    __slots__ = ("t", "w", "r", "name")

    def __init__(self, t, name=""):
        self.t = t
        self.w = None
        self.r = []
        self.name = name

    def __getitem__(self, idx):
        return self.t[idx]

    def ap(self):
        return self.t.ap()


def col_bufs(buf, n):
    return [Buf(buf.t, f"{buf.name}.c{i}") for i in range(n)]


class Bld:
    def __init__(self):
        self.streams = {e: [] for e in ENGINES}
        self.dma_cum = {}
        self.fence_deps = None
        self.fenced = set()
        self.n = 0

    def _dep_tok(self, dep):
        if isinstance(dep, Plan):
            if dep.is_dma:
                return ("D" + dep.stream, self.dma_cum[dep.stream])
            dep.want = True
            return dep
        return dep

    def op(self, eng, fn, reads=(), writes=(), dbg=""):
        p = Plan(eng, fn, dbg=dbg)
        deps = []
        if self.fence_deps is not None and eng not in self.fenced:
            deps.extend(self.fence_deps)
            self.fenced.add(eng)
        for b in reads:
            if b.w is not None:
                deps.append(b.w)
        for b in writes:
            if b.w is not None:
                deps.append(b.w)
            deps.extend(b.r)
        if eng == "tensor":
            deps = [d for d in deps
                    if not (isinstance(d, Plan) and not d.is_dma and d.eng == "tensor")]
        p.waits = [self._dep_tok(d) for d in deps]
        self.streams[eng].append(p)
        for b in reads:
            b.r.append(p)
        for b in writes:
            b.w = p
            b.r = []
        self.n += 1
        return p

    def dma(self, eng, stream, fn, reads=(), writes=(), dbg=""):
        p = Plan(eng, fn, is_dma=True, stream=stream, dbg=dbg)
        deps = []
        if self.fence_deps is not None and eng not in self.fenced:
            deps.extend(self.fence_deps)
            self.fenced.add(eng)
        for b in reads:
            if b.w is not None:
                deps.append(b.w)
        for b in writes:
            if b.w is not None:
                deps.append(b.w)
            deps.extend(b.r)
        p.waits = [self._dep_tok(d) for d in deps]
        prev = self.dma_cum.get(stream, 0)
        if prev:
            p.waits.append(("D" + stream, prev))
        self.streams[eng].append(p)
        cum = prev + 16
        self.dma_cum[stream] = cum
        p.sem = "D" + stream
        p.val = cum
        for b in reads:
            b.r.append(p)
        for b in writes:
            b.w = p
            b.r = []
        self.n += 1
        return p

    def fence(self):
        deps = []
        for e in ENGINES:
            st = self.streams[e]
            for p in reversed(st):
                if not p.is_dma:
                    p.want = True
                    deps.append(p)
                    break
        for sname, cum in self.dma_cum.items():
            deps.append(("D" + sname, cum))
        self.fence_deps = deps
        self.fenced = set()

    def finalize_emit(self, nc, block, es):
        sem_names = set()
        for e in ENGINES:
            cum = 0
            for p in self.streams[e]:
                if p.is_dma:
                    sem_names.add(p.sem)
                elif p.want:
                    cum += 1
                    p.sem = "E" + e
                    p.val = cum
                    sem_names.add(p.sem)
        for p in (q for e in ENGINES for q in self.streams[e]):
            rw = []
            for w in p.waits:
                if isinstance(w, Plan):
                    assert w.sem is not None, f"dep without sem: {w.dbg}"
                    rw.append((w.sem, w.val))
                else:
                    rw.append(w)
            p.waits = rw
        handles = {s: es.enter_context(nc.semaphore(s)) for s in sorted(sem_names)}

        def emit(engine_attr, ename):
            def body(eng):
                mark = {}
                for p in self.streams[ename]:
                    need = {}
                    for s, v in p.waits:
                        if v > need.get(s, -1):
                            need[s] = v
                    for s, v in sorted(need.items()):
                        if mark.get(s, -1) < v:
                            eng.wait_ge(handles[s], v)
                            mark[s] = v
                    instr = p.fn(eng)
                    if p.sem is not None:
                        instr.then_inc(handles[p.sem], 16 if p.is_dma else 1)
            engine_attr(body)

        emit(block.tensor, "tensor")
        emit(block.vector, "vector")
        emit(block.scalar, "scalar")
        emit(block.gpsimd, "gpsimd")
        emit(block.sync, "sync")



def build_program(phase_a_only=False, n_tiles=NT, debug_taps=False, dbg_stage=99):
    nc = bass.Bass("TRN2", target_bir_lowering=False)
    B = Bld()

    dram = {}
    for name, shape, dt in [
        ("x", [R_CORE + 8, D], F32),
        ("wq", [D, D], F32), ("wk", [D, D], F32), ("wv", [D, D], F32), ("wo", [D, D], F32),
        ("rw", [NEXP, D], F32), ("rb", [NEXP], F32),
        ("upw", [NEXP, D, EDIM], F32), ("gw", [NEXP, D, EDIM], F32),
        ("dw", [NEXP, EDIM, D], F32),
        ("shgT", [D, D], F32), ("shuT", [D, D], F32), ("shdT", [D, D], F32),
        ("ident", [128, 128], F32),
        ("sel", [32, NH * 128], BF16),
        ("bmask", [128, 136], F32), ("fmask", [128, 136], F32),
        ("mT1", [128, 128], F32), ("mT0", [8, 128], F32), ("mT0f", [8, 128], F32),
    ]:
        dram[name] = Buf(nc.dram_tensor(name, shape, dt, kind="ExternalInput").ap().tensor, name)
    out_h = Buf(nc.dram_tensor("out", [R_CORE, D], F32, kind="ExternalOutput").ap().tensor, "out")
    x1h = Buf(nc.dram_tensor("x1h", [R_CORE, D], F32).ap().tensor, "x1h")
    chT = Buf(nc.dram_tensor("chT", [NEXP, TILE], F32).ap().tensor, "chT")

    es = ExitStack()

    def sbuf(name, shape, dt):
        return Buf(es.enter_context(nc.sbuf_tensor(name, shape, dt)), name)

    def psum(name, shape, dt=F32):
        return Buf(es.enter_context(nc.psum_tensor(name, shape, dt)), name)

    block = es.enter_context(nc.Block())

    xblk = [sbuf(f"xblk{i}", [128, 1024], F32) for i in range(NB)]
    ln_tok = sbuf("ln_tok", [128, 4 * 1024], BF16)
    ln_slot = col_bufs(ln_tok, 4)
    ln_feat = sbuf("ln_feat", [128, NKC * LNW], BF16)
    ident = sbuf("ident_sb", [128, 128], F32)
    sel = sbuf("sel_sb", [32, NH * 128], BF16)
    rwT = sbuf("rwT", [128, NKC * 16], BF16)
    rb_bc = sbuf("rb_bc", [128, NEXP], F32)
    neg1 = sbuf("neg1", [128, 1], F32)
    ssum = sbuf("ssum", [128, NB], F32)
    ssq = sbuf("ssq", [128, NB], F32)
    mean_a = sbuf("mean_a", [128, NB], F32)
    var_a = sbuf("var_a", [128, NB], F32)
    rstd_a = sbuf("rstd_a", [128, NB], F32)
    msq = sbuf("msq", [128, NB], F32)
    ssum_h = col_bufs(ssum, 2)
    ssq_h = col_bufs(ssq, 2)
    mean_h = col_bufs(mean_a, 2)
    var_h = col_bufs(var_a, 2)
    rstd_h = col_bufs(rstd_a, 2)
    msq_h = col_bufs(msq, 2)

    PA = [psum(f"PA{i}", [128, 512], F32) for i in range(4)]
    PB = [psum(f"PB{i}", [128, 1024], F32) for i in range(2)]

    def t_op(fn, reads=(), writes=(), dbg=""):
        return B.op("tensor", fn, reads, writes, dbg)

    def v_op(fn, reads=(), writes=(), dbg=""):
        return B.op("vector", fn, reads, writes, dbg)

    def s_op(fn, reads=(), writes=(), dbg=""):
        return B.op("scalar", fn, reads, writes, dbg)

    def ydma(stream, fn, reads=(), writes=(), dbg=""):
        return B.dma("sync", stream, fn, reads, writes, dbg)

    def sdma(stream, fn, reads=(), writes=(), dbg=""):
        return B.dma("scalar", stream, fn, reads, writes, dbg)

    def gdma(stream, fn, reads=(), writes=(), dbg=""):
        return B.dma("gpsimd", stream, fn, reads, writes, dbg)

    xchain = Buf(None, "xchain")

    def tp_dma(eng_dma, stream, fn, reads=(), writes=(), dbg=""):
        return eng_dma(stream, fn, tuple(reads) + (xchain,), tuple(writes), dbg)

    def sb2sb_dma(eng_dma, stream, fn, reads=(), writes=(), dbg=""):
        return eng_dma(stream, fn, tuple(reads), tuple(writes) + (xchain,), dbg)

    ssum_c = None

    def ln_sums(blks, b, width, dst_tok):
        sb_ = blks[b]
        dsl = slice((b % 4) * width, (b % 4) * width + width)
        v_op(lambda e, x=sb_, b=b: e.reduce_sum(ssum[:, b:b + 1], x[:], axis=AX),
             [sb_], [ssum_h[b // 4]], "ln_sum")
        s_op(lambda e, x=sb_, d=dsl, b=b: e.activation(dst_tok[:, d], x[:],
                                                       ACT.Square, accum_out=ssq[:, b:b + 1]),
             [sb_], [ln_slot[b % 4], ssq_h[b // 4]], "ln_sq")

    def ln_stats_half(hf, width):
        c = slice(hf * 4, hf * 4 + 4)
        v_op(lambda e: e.tensor_scalar(mean_a[:, c], ssum[:, c], 1.0 / width, None, ALU.mult),
             [ssum_h[hf]], [mean_h[hf]], "ln_mean")
        v_op(lambda e: e.tensor_scalar(var_a[:, c], ssq[:, c], 1.0 / width, None, ALU.mult),
             [ssq_h[hf]], [var_h[hf]], "ln_var0")
        v_op(lambda e: e.tensor_tensor(msq[:, c], mean_a[:, c], mean_a[:, c], ALU.mult),
             [mean_h[hf]], [msq_h[hf]], "ln_msq")
        v_op(lambda e: e.tensor_tensor(var_a[:, c], var_a[:, c], msq[:, c], ALU.subtract),
             [var_h[hf], msq_h[hf]], [var_h[hf]], "ln_var1")
        v_op(lambda e: e.tensor_scalar(var_a[:, c], var_a[:, c], 1e-5, None, ALU.add),
             [var_h[hf]], [var_h[hf]], "ln_eps")
        s_op(lambda e: e.activation(rstd_a[:, c], var_a[:, c], ACT.Ln),
             [var_h[hf]], [rstd_h[hf]], "ln_ln")
        v_op(lambda e: e.tensor_scalar(rstd_a[:, c], rstd_a[:, c], -0.5, None, ALU.mult),
             [rstd_h[hf]], [rstd_h[hf]], "ln_half")
        s_op(lambda e: e.activation(rstd_a[:, c], rstd_a[:, c], ACT.Exp),
             [rstd_h[hf]], [rstd_h[hf]], "ln_exp")

    def ln_pass(blks, nblocks, width, dst_tok, stats_cols):
        for hf in range(2):
            for b in range(hf * 4, hf * 4 + 4):
                ln_sums(blks, b, width, dst_tok)
            ln_stats_half(hf, width)


    lf4 = ln_feat.ap().rearrange("p (c w) -> p c w", c=NKC)

    def ln_apply_tr(blks, b):
        sb_ = blks[b]
        dsl = slice((b % 4) * 1024, (b % 4) * 1024 + 1024)
        v_op(lambda e, x=sb_, d=dsl, b=b: e.tensor_scalar(
            ln_tok[:, d], x[:], mean_a[:, b:b + 1], rstd_a[:, b:b + 1],
            ALU.subtract, ALU.mult),
            [sb_, mean_h[b // 4], rstd_h[b // 4]], [ln_slot[b % 4]], "ln_apply")
        dst = lf4[:, :, MOF + 128 * b: MOF + 128 * b + 128]
        tp_dma(ydma, "lnT%d" % b,
               lambda e, d=dst, ds=dsl: e.dma_start(d, ln_tok[:, ds], transpose=True),
               [ln_slot[b % 4]], [ln_feat], "lnT")

    def ln_apply_transpose(blks, nblocks):
        for b in range(nblocks):
            ln_apply_tr(blks, b)

    a_es = ExitStack()

    def a_sbuf(name, shape, dt):
        return Buf(a_es.enter_context(nc.sbuf_tensor(name, shape, dt)), name)

    for nm, buf in [("ident", ident), ("sel", sel)]:
        gdma("wsh", lambda e, n=nm, bb=buf: e.dma_start(bb[:], dram[n][:]), [dram[nm]], [buf], "c_" + nm)
    B.op("vector", lambda e: e.memset(neg1[:], -1.0), [], [neg1], "neg1")
    masks = {}
    for nm, shape in [("bmask", [128, 136]), ("fmask", [128, 136]),
                      ("mT1", [128, 128]), ("mT0", [8, 128]), ("mT0f", [8, 128])]:
        masks[nm] = sbuf(nm + "_sb", shape, F32)
        gdma("wsh", lambda e, n=nm: e.dma_start(masks[n][:], dram[n][:]), [dram[nm]], [masks[nm]], "c_" + nm)
    rb_src = bass.AP(dram["rb"].t, 0, [[0, 128], [1, NEXP]])
    gdma("wsh", lambda e: e.dma_start(rb_bc[:], rb_src), [dram["rb"]], [rb_bc], "c_rb")
    rw_f32 = a_sbuf("rw_f32", [16, 1024], F32)
    gdma("wsh", lambda e: e.dma_start(rw_f32[:], dram["rw"][:]), [dram["rw"]], [rw_f32], "c_rw")
    for kc in range(NKC):
        t_op(lambda e, k=kc: e.matmul(PA[0][0:128, 0:16], rw_f32[0:16, k * 128:(k + 1) * 128],
                                      ident[0:16, 0:16], start=True, stop=True, is_transpose=True),
             [rw_f32, ident], [PA[0]], "rwT_t")
        v_op(lambda e, k=kc: e.tensor_copy(rwT[:, k * 16:(k + 1) * 16], PA[0][0:128, 0:16]),
             [PA[0]], [rwT], "rwT_c")

    x_halo = a_sbuf("x_halo", [8, 1024], F32)
    q_feat = a_sbuf("q_feat", [128, 8 * 1024], BF16)
    k_feat = a_sbuf("k_feat", [128, 8 * LNW], BF16)
    vT = a_sbuf("vT", [128, NB * 1024], BF16)
    vhalo_l = [a_sbuf("vhalo0", [8, 8 * 1024], BF16)]
    attn_flat = a_sbuf("attn_flat", [128, 8 * 1024], BF16)
    WA = [a_sbuf(f"WA{i}", [128, NKC * 1024], BF16) for i in range(2)]
    hsums = a_sbuf("hsums", [8, 8], F32)
    negm_l = [a_sbuf(f"negm{i}", [128, NH], F32) for i in range(8)]
    negmc_l = [col_bufs(negm_l[i], NH) for i in range(8)]
    rall_l = [a_sbuf(f"rall{i}", [128, NH], F32) for i in range(8)]
    rallc_l = [col_bufs(rall_l[i], NH) for i in range(8)]
    biasq_l = [a_sbuf(f"biasq{i}", [128, NH], F32) for i in range(8)]
    chib_l = [a_sbuf(f"chib{i}", [128, NH], BF16) for i in range(1)]
    pack_l = [a_sbuf(f"pack{i}", [128, 32], F32) for i in range(8)]
    ctb_l = [a_sbuf(f"ctb{i}", [32, 128], BF16) for i in range(8)]
    sm_l = [a_sbuf(f"sm{i}", [128, 136], F32) for i in range(NH)]
    wts1_l = [a_sbuf(f"wts1{i}", [128, 256], BF16) for i in range(4)]
    wts0_l = [a_sbuf(f"wts0{i}", [8, 256], BF16) for i in range(4)]

    kf4 = k_feat.ap().rearrange("p (c w) -> p c w", c=NKC)
    qf4 = q_feat.ap().rearrange("p (c w) -> p c w", c=NKC)

    vhalo = None

    def vh_ap(q, lo, hi):
        return vhalo[0:8, q * 1024 + lo: q * 1024 + hi]

    def load_wa(slot, name, dbg):
        gdma("wa%d" % slot, lambda e: e.dma_start(
            WA[slot].ap().rearrange("p (c w) -> p c w", c=NKC),
            dram[name].ap().rearrange("(c p) w -> p c w", p=128)),
            [dram[name]], [WA[slot]], dbg)

    load_wa(0, "wq", "wq0")
    load_wa(1, "wk", "wk0")

    def phase_a_tile(t):
        nonlocal vhalo
        vhalo = vhalo_l[0]
        t0 = t * TILE
        if t > 0:
            v_op(lambda e: e.tensor_copy(kf4[:, :, HOF:HOF + 8], kf4[:, :, MOF + 1016:MOF + 1024]),
                 [k_feat], [k_feat], "khalo_cp")
        for b in range(NB):
            ydma("dx%d" % b, lambda e, bb=b: e.dma_start(
                xblk[bb].ap().rearrange("p w -> p w"),
                dram["x"].ap()[8 + t0 + 128 * bb: 8 + t0 + 128 * (bb + 1), :]),
                [dram["x"]], [xblk[b]], f"x{t}_{b}")
        if t == 0:
            ydma("dx", lambda e: e.dma_start(x_halo[:], dram["x"].ap()[t0:t0 + 8, :]),
                 [dram["x"]], [x_halo], "xh")
        for hf in range(2):
            for b in range(hf * 4, hf * 4 + 4):
                ln_sums(xblk, b, 1024, ln_tok)
            ln_stats_half(hf, 1024)
            for b in range(hf * 4, hf * 4 + 4):
                ln_apply_tr(xblk, b)
        if t > 0:
            sb2sb_dma(ydma, "vh0", lambda e: e.dma_start(
                vh_ap(0, 0, 1024), vT[120:128, 7 * 1024:8 * 1024]),
                [vT], [vhalo], "vhalo0")
        if dbg_stage < 1:
            return
        if t == 0:
            v_op(lambda e: e.reduce_sum(hsums[:, 0:1], x_halo[:], axis=AX), [x_halo], [hsums], "hln")
            s_op(lambda e: e.activation(ln_tok[0:8, 0:1024], x_halo[:], ACT.Square,
                                        accum_out=hsums[:, 1:2]),
                 [x_halo], [ln_slot[0], hsums], "hsq")
            v_op(lambda e: e.tensor_scalar(hsums[:, 2:3], hsums[:, 0:1], 1.0 / 1024, None, ALU.mult),
                 [hsums], [hsums], "hm")
            v_op(lambda e: e.tensor_scalar(hsums[:, 3:4], hsums[:, 1:2], 1.0 / 1024, None, ALU.mult),
                 [hsums], [hsums], "hv")
            v_op(lambda e: e.tensor_tensor(hsums[:, 4:5], hsums[:, 2:3], hsums[:, 2:3], ALU.mult),
                 [hsums], [hsums], "hmsq")
            v_op(lambda e: e.tensor_tensor(hsums[:, 3:4], hsums[:, 3:4], hsums[:, 4:5], ALU.subtract),
                 [hsums], [hsums], "hv2")
            v_op(lambda e: e.tensor_scalar(hsums[:, 3:4], hsums[:, 3:4], 1e-5, None, ALU.add),
                 [hsums], [hsums], "heps")
            s_op(lambda e: e.activation(hsums[:, 5:6], hsums[:, 3:4], ACT.Ln), [hsums], [hsums], "hln2")
            v_op(lambda e: e.tensor_scalar(hsums[:, 5:6], hsums[:, 5:6], -0.5, None, ALU.mult),
                 [hsums], [hsums], "hhalf")
            s_op(lambda e: e.activation(hsums[:, 5:6], hsums[:, 5:6], ACT.Exp), [hsums], [hsums], "hexp")
            v_op(lambda e: e.tensor_scalar(x_halo[:], x_halo[:], hsums[:, 2:3], hsums[:, 5:6],
                                           ALU.subtract, ALU.mult),
                 [x_halo, hsums], [x_halo], "happly")
            for kc in range(NKC):
                t_op(lambda e, k=kc: e.matmul(PA[0][0:128, k * 8:(k + 1) * 8],
                                              x_halo[0:8, k * 128:(k + 1) * 128],
                                              ident[0:8, 0:8], start=True, stop=True,
                                              is_transpose=True),
                     [x_halo, ident], [PA[0]], "hT")
            v_op(lambda e: e.tensor_copy(
                lf4[:, :, HOF:HOF + 8],
                bass.AP(PA[0].t, 0, [[PA[0].ap().ap[0][0], 128], [8, 8], [1, 8]])),
                [PA[0]], [ln_feat], "hTc")

        for nh in range(2):
            for mc in range(NKC):
                ps = PA[(mc * 2 + nh) % 2]
                for kc in range(NKC):
                    t_op(lambda e, m=mc, n=nh, k=kc, p=ps: e.matmul(
                        p[:, 0:512],
                        WA[0][:, k * 1024 + m * 128: k * 1024 + (m + 1) * 128],
                        lf4[:, k, MOF + n * 512: MOF + (n + 1) * 512],
                        start=(k == 0), stop=(k == NKC - 1)),
                        [WA[0], ln_feat], [ps], "qmm")
                v_op(lambda e, m=mc, n=nh, p=ps: e.tensor_scalar(
                    qf4[:, m, n * 512:(n + 1) * 512], p[:, 0:512], SCALE, None, ALU.mult),
                    [ps], [q_feat], "qev")
        load_wa(0, "wv", f"wv{t}")
        for nh in range(2):
            for mc in range(NKC):
                ps = PA[2 + (mc * 2 + nh) % 2]
                for kc in range(NKC):
                    t_op(lambda e, m=mc, n=nh, k=kc, p=ps: e.matmul(
                        p[:, 0:512],
                        WA[1][:, k * 1024 + m * 128: k * 1024 + (m + 1) * 128],
                        lf4[:, k, MOF + n * 512: MOF + (n + 1) * 512],
                        start=(k == 0), stop=(k == NKC - 1)),
                        [WA[1], ln_feat], [ps], "kmm")
                v_op(lambda e, m=mc, n=nh, p=ps: e.tensor_copy(
                    kf4[:, m, MOF + n * 512: MOF + (n + 1) * 512], p[:, 0:512]),
                    [ps], [k_feat], "kev")
        if t == 0:
            for mc in range(NKC):
                for kc in range(NKC):
                    t_op(lambda e, m=mc, k=kc: e.matmul(
                        PA[0][:, m * 8:(m + 1) * 8],
                        WA[1][:, k * 1024 + m * 128: k * 1024 + (m + 1) * 128],
                        lf4[:, k, HOF:HOF + 8], start=(k == 0), stop=(k == NKC - 1)),
                        [WA[1], ln_feat], [PA[0]], "khmm")
            v_op(lambda e: e.tensor_copy(
                kf4[:, :, HOF:HOF + 8],
                bass.AP(PA[0].t, 0, [[PA[0].ap().ap[0][0], 128], [8, 8], [1, 8]])),
                [PA[0]], [k_feat], "khev")
        load_wa(1, "wo", f"wo{t}")
        stats_q = []

        def drain_stat():
            if stats_q:
                stats_q.pop(0)()

        def pass1_waves(q):
            nonlocal stats_q
            qp = q
            negm, rall, biasq = negm_l[qp], rall_l[qp], biasq_l[qp]
            negmc, rallc = negmc_l[qp], rallc_l[qp]
            lnr = rall
            chib, pack, ctb = chib_l[0], pack_l[qp], ctb_l[qp]
            first = (t == 0 and q == 0)
            p1mask = masks["fmask"] if first else masks["bmask"]
            for h in range(NH):
                ch, off = h // 2, 64 * (h % 2)
                sps = PA[2 + h % 2]
                sm = sm_l[h]
                t_op(lambda e, c=ch, o=off, qq=q, p=sps: e.matmul(
                    p[:, 0:136],
                    qf4[o:o + 64, c, qq * 128:(qq + 1) * 128],
                    kf4[o:o + 64, c, HOF + qq * 128: HOF + qq * 128 + 136],
                    start=True, stop=True),
                    [q_feat, k_feat], [sps], "p1mm")
                v_op(lambda e, p=sps, s=sm, mk=p1mask: e.tensor_tensor(
                    s[:], p[:, 0:136], mk[:], ALU.add), [sps, p1mask], [sm], "p1mask")
                if h % 3 == 0:
                    drain_stat()
            for h in range(NH):
                sm = sm_l[h]
                v_op(lambda e, s=sm, hh=h, n=negm: e.reduce_max(
                    n[:, hh:hh + 1], s[:], axis=AX, negate=True), [sm], [negmc[h]], "p1max")
            for h in range(NH):
                sm = sm_l[h]
                s_op(lambda e, s=sm, hh=h, n=negm, r=rall: e.activation(
                    s[:], s[:], ACT.Exp, bias=n[:, hh:hh + 1],
                    accum_out=r[:, hh:hh + 1]),
                    [sm, negmc[h]], [sm, rallc[h]], "p1exp")
            s_op(lambda e, r=rall, l=lnr: e.activation(l[:], r[:], ACT.Ln),
                 list(rallc), list(rallc), "lnr")

            def _mk(qq, negm=negm, lnr=lnr, biasq=biasq, chib=chib, pack=pack, ctb=ctb,
                    negmc=negmc, rallc=rallc):
                return [
                    lambda: v_op(lambda e: e.tensor_tensor(biasq[:], negm[:], lnr[:], ALU.subtract),
                                 list(negmc) + list(rallc), [biasq], "bias"),
                    lambda: v_op(lambda e: e.tensor_copy(chib[:], biasq[:]), [biasq], [chib], "chib"),
                    lambda: v_op(lambda e: e.tensor_copy(pack[:, 0:16], chib[:]), [chib], [pack], "chif"),
                    lambda: v_op(lambda e: e.tensor_tensor(pack[:, 16:32], biasq[:], pack[:, 0:16],
                                                           ALU.subtract), [biasq, pack], [pack], "clo"),
                ]
            assert not stats_q
            stats_q = _mk(q)

        def ct_emit(q):
            pack, ctb = pack_l[q], ctb_l[q]
            tps = PA[2 + q % 2]
            t_op(lambda e, pk=pack, p=tps: e.matmul(
                p[0:32, 0:128], pk[:], ident[:], start=True, stop=True, is_transpose=True),
                [pack, ident], [tps], "cT")
            v_op(lambda e, p=tps, c=ctb: e.tensor_copy(c[:], p[0:32, 0:128]), [tps], [ctb], "cTc")
        for b in range(NB):
            ps = PB[b % 2]
            for kc in range(NKC):
                for nh in range(2):
                    t_op(lambda e, bb=b, k=kc, n=nh, p=ps: e.matmul(
                        p[:, n * 512:(n + 1) * 512],
                        lf4[:, k, MOF + bb * 128: MOF + (bb + 1) * 128],
                        WA[0][:, k * 1024 + n * 512: k * 1024 + (n + 1) * 512],
                        start=(k == 0), stop=(k == NKC - 1)),
                        [ln_feat, WA[0]], [ps], "vmm")
            v_op(lambda e, bb=b, p=ps: e.tensor_copy(vT[:, bb * 1024:(bb + 1) * 1024], p[:]),
                 [ps], [vT], "vev")
        if t == 0:
            for kc in range(NKC):
                for nh in range(2):
                    t_op(lambda e, k=kc, n=nh: e.matmul(
                        PB[0][0:8, n * 512:(n + 1) * 512],
                        lf4[:, k, HOF:HOF + 8],
                        WA[0][:, k * 1024 + n * 512: k * 1024 + (n + 1) * 512],
                        start=(k == 0), stop=(k == NKC - 1)),
                        [ln_feat, WA[0]], [PB[0]], "vhmm")
            v_op(lambda e: e.tensor_copy(vh_ap(0, 0, 1024), PB[0][0:8, :]), [PB[0]], [vhalo], "vhev")
        if t + 1 < n_tiles:
            load_wa(0, "wq", f"wq{t + 1}")
        for q in range(1, 8):
            sb2sb_dma(ydma, "vh%d" % (1 + (q % 2)), lambda e, qq=q: e.dma_start(
                vh_ap(qq, 0, 1024), vT[120:128, (qq - 1) * 1024: qq * 1024]),
                [vT], [vhalo], f"vhalo{q}")

        if dbg_stage < 2:
            return
        def pass2_pipe(q):
            ctb = ctb_l[q]
            first = (t == 0 and q == 0)
            m0mask = masks["mT0f"] if first else masks["mT0"]

            def p2_mms(m, q=q, ctb=ctb):
                pp = PA[m % 2]
                for i in range(2):
                    h = 2 * m + i
                    ch, off = h // 2, 64 * (h % 2)
                    t_op(lambda e, c=ch, o=off, qq=q, p=pp, ii=i: e.matmul(
                        p[:, ii * 128:(ii + 1) * 128],
                        kf4[o:o + 64, c, MOF + qq * 128: MOF + qq * 128 + 128],
                        qf4[o:o + 64, c, qq * 128:(qq + 1) * 128],
                        start=True, stop=False),
                        [k_feat, q_feat], [pp], "p2mm1")
                    t_op(lambda e, hh=h, p=pp, ii=i, c=ctb: e.matmul(
                        p[:, ii * 128:(ii + 1) * 128],
                        sel[:, hh * 128:(hh + 1) * 128],
                        c[:], start=False, stop=True),
                        [sel, ctb], [pp], "p2mmc1")
                    t_op(lambda e, c=ch, o=off, qq=q, p=pp, ii=i: e.matmul(
                        p[0:8, 256 + ii * 128: 256 + (ii + 1) * 128],
                        kf4[o:o + 64, c, HOF + qq * 128: HOF + qq * 128 + 8],
                        qf4[o:o + 64, c, qq * 128:(qq + 1) * 128],
                        start=True, stop=False),
                        [k_feat, q_feat], [pp], "p2mm0")
                    t_op(lambda e, hh=h, p=pp, ii=i, c=ctb: e.matmul(
                        p[0:8, 256 + ii * 128: 256 + (ii + 1) * 128],
                        sel[:, hh * 128: hh * 128 + 8],
                        c[:], start=False, stop=True),
                        [sel, ctb], [pp], "p2mmc0")

            def p2_madds(m, m0mask=m0mask):
                pp = PA[m % 2]
                mT1b = bass.AP(masks["mT1"].t, 0, [[masks["mT1"].ap().ap[0][0], 128], [0, 2], [1, 128]])
                mT0b = bass.AP(m0mask.t, 0, [[m0mask.ap().ap[0][0], 8], [0, 2], [1, 128]])
                v_op(lambda e, p=pp, mb=mT1b: e.tensor_tensor(
                    p.ap()[:, 0:256].rearrange("p (r w) -> p r w", r=2),
                    p.ap()[:, 0:256].rearrange("p (r w) -> p r w", r=2), mb, ALU.add),
                    [pp, masks["mT1"]], [pp], "p2m1")
                v_op(lambda e, p=pp, mb=mT0b: e.tensor_tensor(
                    p.ap()[0:8, 256:512].rearrange("p (r w) -> p r w", r=2),
                    p.ap()[0:8, 256:512].rearrange("p (r w) -> p r w", r=2), mb, ALU.add),
                    [pp, m0mask], [pp], "p2m0")

            def p2_exps(m):
                pp = PA[m % 2]
                wts1, wts0 = wts1_l[m % 4], wts0_l[m % 4]
                s_op(lambda e, p=pp, w=wts1: e.activation(w[:], p[:, 0:256], ACT.Exp),
                     [pp], [wts1], "p2exp1")
                s_op(lambda e, p=pp, w=wts0: e.activation(w[:], p[0:8, 256:512], ACT.Exp),
                     [pp], [wts0], "p2exp0")

            def p2_av(m, q=q):
                wts1, wts0 = wts1_l[m % 4], wts0_l[m % 4]
                avp = PB[m % 2]
                for i in range(2):
                    h = 2 * m + i
                    off = 64 * (h % 2)
                    tp = None if off == 0 else (0, 64)
                    t_op(lambda e, hh=h, o=off, qq=q, p=avp, w=wts1, ii=i, tpos=tp: e.matmul(
                        p[o:o + 64, 0:128],
                        vT[:, qq * 1024 + hh * 64: qq * 1024 + (hh + 1) * 64],
                        w[:, ii * 128:(ii + 1) * 128],
                        start=True, stop=False, tile_position=tpos),
                        [vT, wts1], [avp], "avmm1")
                    t_op(lambda e, hh=h, o=off, qq=q, p=avp, w=wts0, ii=i, tpos=tp: e.matmul(
                        p[o:o + 64, 0:128],
                        vh_ap(qq, hh * 64, (hh + 1) * 64),
                        w[:, ii * 128:(ii + 1) * 128],
                        start=False, stop=True, tile_position=tpos),
                        [vhalo, wts0], [avp], "avmm0")

            def p2_avev(m, q=q):
                avp = PB[m % 2]
                v_op(lambda e, p=avp, mm=m, qq=q: e.tensor_copy(
                    attn_flat[:, mm * 1024 + qq * 128: mm * 1024 + qq * 128 + 128],
                    p[:, 0:128]), [avp], [attn_flat], "avev")

            for step in range(NH // 2 + 3):
                if step < 8:
                    p2_mms(step)
                    p2_madds(step)
                if 1 <= step < 9:
                    p2_exps(step - 1)
                if 2 <= step < 10:
                    p2_av(step - 2)
                if 3 <= step:
                    p2_avev(step - 3)
            b = q
            af4 = attn_flat.ap().rearrange("p (c w) -> p c w", c=NKC)
            ps = PB[(b + 1) % 2]
            for kc in range(NKC):
                for nh in range(2):
                    t_op(lambda e, bb=b, k=kc, n=nh, p=ps: e.matmul(
                        p[:, n * 512:(n + 1) * 512],
                        af4[:, k, bb * 128:(bb + 1) * 128],
                        WA[1][:, k * 1024 + n * 512: k * 1024 + (n + 1) * 512],
                        start=(k == 0), stop=(k == NKC - 1)),
                        [attn_flat, WA[1]], [ps], "womm")
            v_op(lambda e, bb=b, p=ps: e.tensor_tensor(
                xblk[bb][:], p[:], xblk[bb][:], ALU.add),
                [ps, xblk[b]], [xblk[b]], "woev")
            sdma("x1w%d" % b, lambda e, bb=b: e.dma_start(
                x1h.ap()[t0 + 128 * bb: t0 + 128 * (bb + 1), :], xblk[bb].ap()),
                [xblk[b]], [x1h], f"x1w{t}_{b}")
            if phase_a_only:
                sdma("dout%d" % b, lambda e, bb=b: e.dma_start(
                    out_h.ap()[t0 + 128 * bb: t0 + 128 * (bb + 1), :], xblk[bb].ap()),
                    [xblk[b]], [out_h], f"outA{t}_{b}")

        pass1_waves(0)
        for q in range(NB):
            if q + 1 < NB:
                pass1_waves(q + 1)
            else:
                while stats_q:
                    drain_stat()
            ct_emit(q)
            pass2_pipe(q)
        if t + 1 < n_tiles:
            load_wa(1, "wk", f"wk{t + 1}")

    for t in range(n_tiles):
        phase_a_tile(t)
        if debug_taps and t == 0:
            for nm, buf, shape in [("d_lnf", ln_feat, [128, NKC * LNW]),
                                   ("d_lnt", ln_tok, [128, 4 * 1024]),
                                   ("d_q", q_feat, [128, 8 * 1024]),
                                   ("d_k", k_feat, [128, 8 * LNW]),
                                   ("d_vT", vT, [128, NB * 1024]),
                                   ("d_vh", vhalo, [8, 8 * 1024]),
                                   ("d_af", attn_flat, [128, 8 * 1024])]:
                dh = Buf(nc.dram_tensor(nm, shape, F32, kind="ExternalOutput").ap().tensor, nm)
                gdma("dbg", lambda e, dd=dh, bb=buf: e.dma_start(dd.ap(), bb.ap()),
                     [buf], [dh], "tap_" + nm)

    if phase_a_only:
        B.op("vector", lambda e: e.engine_nop(), [out_h], [], "fin")
        B.finalize_emit(nc, block, es)
        a_es.close()
        es.close()
        nc.finalize()
        return nc

    a_es.close()
    B.fence()

    b_es = ExitStack()

    def b_sbuf(name, shape, dt):
        return Buf(b_es.enter_context(nc.sbuf_tensor(name, shape, dt)), name)

    lg_sb = b_sbuf("lg_sb", [128, NB * 16], F32)
    c_sb = b_sbuf("c_sb", [128, NB * 16], F32)
    m1 = b_sbuf("m1", [128, NB], F32)
    m2 = b_sbuf("m2", [128, NB], F32)
    p1 = b_sbuf("p1", [128, NB], F32)
    p2 = b_sbuf("p2", [128, NB], F32)
    dif = b_sbuf("dif", [128, NB], F32)
    eq1 = b_sbuf("eq1", [128, 16], F32)
    eq2 = b_sbuf("eq2", [128, 16], F32)
    lgm = b_sbuf("lgm", [128, 16], F32)
    cT_sb = b_sbuf("cT_sb", [16, NB * 128], F32)
    c_ball = b_sbuf("c_ball", [128, 8 * 1024], BF16)
    h_g = b_sbuf("h_g", [128, 4 * 4 * 1024], BF16)
    hg_slot = col_bufs(h_g, 4)
    u_sb = b_sbuf("u_sb", [128, 4 * 512], BF16)
    gs_sb = b_sbuf("gs_sb", [128, 4 * 512], BF16)
    UPG = [b_sbuf(f"UPG{i}", [128, NKC * 1024], BF16) for i in range(2)]
    DWN = [b_sbuf(f"DWN{i}", [128, 4 * 1024], BF16) for i in range(4)]

    def load_upg(slot, e_idx, dbg):
        if e_idx < NEXP:
            up_src = dram["upw"].ap()[e_idx].rearrange("(c p) w -> p c w", p=128)
            g_src = dram["gw"].ap()[e_idx].rearrange("(c p) w -> p c w", p=128)
        else:
            k = e_idx - NEXP
            up_src = dram["shuT"].ap().rearrange("(c p) w -> p c w", p=128)[:, :, k * 512:(k + 1) * 512]
            g_src = dram["shgT"].ap().rearrange("(c p) w -> p c w", p=128)[:, :, k * 512:(k + 1) * 512]
        dst = UPG[slot].ap().rearrange("p (c t w) -> p c t w", c=NKC, t=2)
        rd = [dram["upw"], dram["gw"]] if e_idx < NEXP else [dram["shuT"], dram["shgT"]]
        gdma("wu%d" % slot, lambda e: e.dma_start(dst[:, :, 0, :], up_src), rd[:1], [UPG[slot]], dbg + "u")
        gdma("wu%d" % slot, lambda e: e.dma_start(dst[:, :, 1, :], g_src), rd[1:], [UPG[slot]], dbg + "g")

    def load_dwn(slot, e_idx, dbg):
        if e_idx < NEXP:
            src = dram["dw"].ap()[e_idx].rearrange("(c p) w -> p c w", p=128)
            rd = [dram["dw"]]
        else:
            k = e_idx - NEXP
            src = dram["shdT"].ap()[512 * k: 512 * (k + 1), :].rearrange("(c p) w -> p c w", p=128)
            rd = [dram["shdT"]]
        gdma("wd%d" % slot, lambda e: e.dma_start(
            DWN[slot].ap().rearrange("p (c w) -> p c w", c=4), src), rd, [DWN[slot]], dbg)

    b_head_done = set()

    def b_head(t, b):
        t0 = t * TILE
        ydma("x1r%d" % b, lambda e, bb=b: e.dma_start(
            xblk[bb].ap(), x1h.ap()[t0 + 128 * bb: t0 + 128 * (bb + 1), :]),
            [x1h], [xblk[b]], f"x1r{t}_{b}")
        b_head_done.add((t, b))

    ln_hoisted = set()

    def ln_sums_s(b):
        sb_ = xblk[b]
        dsl = slice((b % 4) * 1024, (b % 4) * 1024 + 1024)
        s_op(lambda e, x=sb_, bb=b: e.activation(u_sb[:, 0:1024], x[:], ACT.Copy,
                                                 accum_out=ssum[:, bb:bb + 1]),
             [sb_], [u_sb, ssum_h[b // 4]], "ln_sumS")
        s_op(lambda e, x=sb_, d=dsl, bb=b: e.activation(ln_tok[:, d], x[:], ACT.Square,
                                                        accum_out=ssq[:, bb:bb + 1]),
             [sb_], [ln_slot[b % 4], ssq_h[b // 4]], "ln_sqS")

    def phase_b_tile(t):
        t0 = t * TILE
        eseq0 = [e for g in GROUPS for e in g]
        load_upg(0, eseq0[0], "e0")
        load_upg(1, eseq0[1], "e1")
        for sl, e_idx in enumerate(GROUPS[0]):
            load_dwn(sl, e_idx, f"d{e_idx}")
        if t not in ln_hoisted:
            for hf in range(2):
                for b in range(hf * 4, hf * 4 + 4):
                    if (t, b) not in b_head_done:
                        b_head(t, b)
                    ln_sums(xblk, b, 1024, ln_tok)
                ln_stats_half(hf, 1024)
                for b in range(hf * 4, hf * 4 + 4):
                    ln_apply_tr(xblk, b)
        for b in range(NB):
            ps = PA[b % 2]
            for kc in range(NKC):
                t_op(lambda e, bb=b, k=kc, p=ps: e.matmul(
                    p[:, 0:16],
                    lf4[:, k, MOF + bb * 128: MOF + (bb + 1) * 128],
                    rwT[:, k * 16:(k + 1) * 16],
                    start=(k == 0), stop=(k == NKC - 1)),
                    [ln_feat, rwT], [ps], "rtmm")
            v_op(lambda e, bb=b, p=ps: e.tensor_tensor(
                lg_sb[:, bb * 16:(bb + 1) * 16], p[:, 0:16], rb_bc[:], ALU.add),
                [ps, rb_bc], [lg_sb], "rtev")
            v_op(lambda e, bb=b: e.reduce_max(m1[:, bb:bb + 1], lg_sb[:, bb * 16:(bb + 1) * 16], axis=AX),
                 [lg_sb], [m1], "t2m1")
            v_op(lambda e, bb=b: e.tensor_scalar(eq1[:], lg_sb[:, bb * 16:(bb + 1) * 16],
                                                 m1[:, bb:bb + 1], None, ALU.is_equal),
                 [lg_sb, m1], [eq1], "t2eq1")
            v_op(lambda e, bb=b: e.scalar_tensor_tensor(lgm[:], eq1[:], NEG,
                                                        lg_sb[:, bb * 16:(bb + 1) * 16],
                                                        ALU.mult, ALU.add),
                 [eq1, lg_sb], [lgm], "t2lgm")
            v_op(lambda e, bb=b: e.reduce_max(m2[:, bb:bb + 1], lgm[:], axis=AX), [lgm], [m2], "t2m2")
            v_op(lambda e, bb=b: e.tensor_scalar(eq2[:], lgm[:], m2[:, bb:bb + 1], None, ALU.is_equal),
                 [lgm, m2], [eq2], "t2eq2")
            v_op(lambda e, bb=b: e.tensor_tensor(dif[:, bb:bb + 1], m1[:, bb:bb + 1],
                                                 m2[:, bb:bb + 1], ALU.subtract),
                 [m1, m2], [dif], "t2dif")
            s_op(lambda e, bb=b: e.activation(p1[:, bb:bb + 1], dif[:, bb:bb + 1], ACT.Sigmoid),
                 [dif], [p1], "t2p1")
            v_op(lambda e, bb=b: e.tensor_scalar(p2[:, bb:bb + 1], p1[:, bb:bb + 1], -1.0, 1.0,
                                                 ALU.mult, ALU.add), [p1], [p2], "t2p2")
            v_op(lambda e, bb=b: e.tensor_scalar(c_sb[:, bb * 16:(bb + 1) * 16], eq1[:],
                                                 p1[:, bb:bb + 1], None, ALU.mult),
                 [eq1, p1], [c_sb], "t2c1")
            v_op(lambda e, bb=b: e.scalar_tensor_tensor(
                c_sb[:, bb * 16:(bb + 1) * 16], eq2[:], p2[:, bb:bb + 1],
                c_sb[:, bb * 16:(bb + 1) * 16], ALU.mult, ALU.add),
                [eq2, p2, c_sb], [c_sb], "t2c2")
            tps = PA[2 + b % 2]
            t_op(lambda e, bb=b, p=tps: e.matmul(
                p[0:16, 0:128], c_sb[:, bb * 16:(bb + 1) * 16], ident[:],
                start=True, stop=True, is_transpose=True),
                [c_sb, ident], [tps], "cTmm")
            v_op(lambda e, bb=b, p=tps: e.tensor_copy(cT_sb[:, bb * 128:(bb + 1) * 128],
                                                      p[0:16, 0:128]), [tps], [cT_sb], "cTev")
        sdma("cw", lambda e: e.dma_start(chT.ap(), cT_sb.ap()), [cT_sb], [chT], f"cw{t}")
        cb_src0 = bass.AP(chT.t, 0, [[0, 128], [1, 8 * TILE]])
        gdma("cb", lambda e: e.dma_start(c_ball[:], cb_src0), [chT], [c_ball], f"cb{t}a")

        eseq = [e for g in GROUPS for e in g]
        for g_i, group in enumerate(GROUPS):
            if group and group[0] == 8:
                cb_src1 = bass.AP(chT.t, 8 * TILE, [[0, 128], [1, 8 * TILE]])
                gdma("cb", lambda e: e.dma_start(c_ball[:], cb_src1), [chT], [c_ball], f"cb{t}b")
            for ei, e_idx in enumerate(group):
                slot = eseq.index(e_idx) % 2
                shared = e_idx >= NEXP
                for s in range(2):
                    for mc in range(4):
                        pu = PA[mc]
                        for kc in range(NKC):
                            t_op(lambda e, m=mc, k=kc, ss=s, sl=slot, p=pu: e.matmul(
                                p[:, 0:512],
                                UPG[sl][:, k * 1024 + m * 128:k * 1024 + (m + 1) * 128],
                                lf4[:, k, MOF + ss * 512: MOF + (ss + 1) * 512],
                                start=(k == 0), stop=(k == NKC - 1)),
                                [UPG[slot], ln_feat], [pu], "umm")
                        if shared:
                            v_op(lambda e, m=mc, p=pu: e.tensor_copy(
                                u_sb[:, m * 512:(m + 1) * 512], p[:, 0:512]),
                                [pu], [u_sb], "uev_sh")
                        else:
                            v_op(lambda e, m=mc, ss=s, ee=e_idx % 8, p=pu: e.tensor_tensor(
                                u_sb[:, m * 512:(m + 1) * 512], p[:, 0:512],
                                c_ball[:, ee * 1024 + ss * 512: ee * 1024 + (ss + 1) * 512],
                                ALU.mult), [pu, c_ball], [u_sb], "uev")
                    for mg in range(2):
                        pg = PB[mg]
                        for mi in range(2):
                            mc = mg * 2 + mi
                            for kc in range(NKC):
                                t_op(lambda e, m=mc, k=kc, ss=s, sl=slot, p=pg, mm=mi: e.matmul(
                                    p[:, mm * 512:(mm + 1) * 512],
                                    UPG[sl][:, k * 1024 + 512 + m * 128: k * 1024 + 512 + (m + 1) * 128],
                                    lf4[:, k, MOF + ss * 512: MOF + (ss + 1) * 512],
                                    start=(k == 0), stop=(k == NKC - 1)),
                                    [UPG[slot], ln_feat], [pg], "gmm")
                        s_op(lambda e, g=mg, p=pg: e.activation(
                            gs_sb[:, g * 1024:(g + 1) * 1024], p[:], ACT.Silu),
                            [pg], [gs_sb], "silu")
                    hdst = h_g.ap().rearrange("p (e m w) -> p e m w", e=4, m=4)[:, ei, :, s * 512:(s + 1) * 512]
                    v_op(lambda e, d=hdst: e.tensor_tensor(
                        d, u_sb.ap().rearrange("p (m w) -> p m w", m=4),
                        gs_sb.ap().rearrange("p (m w) -> p m w", m=4), ALU.mult),
                        [u_sb, gs_sb], [hg_slot[ei]], "hmul")
                pos = eseq.index(e_idx)
                if pos + 2 < len(eseq):
                    load_upg(slot, eseq[pos + 2], f"e{eseq[pos + 2]}")
            for b in range(NB):
                pd = PB[b % 2]
                nlast = len(group) - 1
                for ei, e_idx in enumerate(group):
                    dslot = ei
                    for hc in range(4):
                        for nh in range(2):
                            first_mm = (ei == 0 and hc == 0)
                            last_mm = (ei == nlast and hc == 3)
                            t_op(lambda e, eii=ei, h=hc, n=nh, bb=b, p=pd, sl=dslot,
                                 st=first_mm, sp=last_mm: e.matmul(
                                     p[:, n * 512:(n + 1) * 512],
                                     h_g.ap().rearrange("p (x m w) -> p x m w", x=4, m=4)[:, eii, h, bb * 128:(bb + 1) * 128],
                                     DWN[sl][:, h * 1024 + n * 512: h * 1024 + (n + 1) * 512],
                                     start=st, stop=sp),
                                [hg_slot[ei], DWN[dslot]], [pd], "dmm")
                v_op(lambda e, bb=b, p=pd: e.tensor_tensor(
                    xblk[bb][:], p[:], xblk[bb][:], ALU.add),
                    [pd, xblk[b]], [xblk[b]], "dacc")
                if g_i + 1 == len(GROUPS):
                    sdma("dout%d" % b, lambda e, bb=b: e.dma_start(
                        out_h.ap()[t0 + 128 * bb: t0 + 128 * (bb + 1), :], xblk[bb].ap()),
                        [xblk[b]], [out_h], f"out{t}_{b}")
                    if t + 1 < n_tiles:
                        b_head(t + 1, b)
                        if b >= 1:
                            ln_sums_s(b - 1)
            if g_i + 1 == len(GROUPS) and t + 1 < n_tiles:
                ln_sums_s(7)
                for hf in range(2):
                    ln_stats_half(hf, 1024)
                for b in range(NB):
                    ln_apply_tr(xblk, b)
                ln_hoisted.add(t + 1)
            if g_i + 1 < len(GROUPS):
                for ei, e_idx in enumerate(GROUPS[g_i + 1]):
                    load_dwn(ei, e_idx, f"d{e_idx}")


    for t in range(n_tiles):
        phase_b_tile(t)

    B.op("vector", lambda e: e.engine_nop(), [out_h], [], "fin")
    B.finalize_emit(nc, block, es)
    b_es.close()
    es.close()
    nc.finalize()
    return nc



def _make_consts():
    i_idx = np.arange(128)[:, None]
    j_idx = np.arange(136)[None, :]
    band = np.where((i_idx <= j_idx) & (j_idx <= i_idx + RAD), 0.0, NEG).astype(np.float32)
    fmask_even = band.copy()
    fmask_even[:, 0:8] = NEG
    jj = np.arange(128)[:, None]
    tt = np.arange(128)[None, :]
    mT1 = np.where((jj <= tt) & (tt <= jj + RAD), 0.0, NEG).astype(np.float32)
    j0 = np.arange(8)[:, None]
    mT0 = np.where(tt <= j0, 0.0, NEG).astype(np.float32)
    ident = np.eye(128, dtype=np.float32)
    sel = np.zeros((32, NH * 128), np.float32)
    for h in range(NH):
        sel[h, h * 128:(h + 1) * 128] = 1.0
        sel[NH + h, h * 128:(h + 1) * 128] = 1.0
    bf = ml_dtypes.bfloat16
    return {
        "band": band, "fmask_even": fmask_even,
        "mT1": mT1, "mT0": mT0,
        "mT0z": np.full_like(mT0, NEG),
        "ident": ident, "sel": sel.astype(bf),
    }


def kernel(L0_attn_x, L0_attn_Wq, L0_attn_Wk, L0_attn_Wv, L0_attn_Wo,
           L0_ffn_router_w, L0_ffn_router_bias, L0_ffn_up_proj, L0_ffn_gate_proj,
           L0_ffn_down_proj, L0_ffn_sh_gate, L0_ffn_sh_up, L0_ffn_sh_down):
    x = np.ascontiguousarray(np.asarray(L0_attn_x, dtype=np.float32))
    Bsz, S, _ = x.shape
    consts = _make_consts()
    common = {
        "wq": np.ascontiguousarray(L0_attn_Wq, np.float32),
        "wk": np.ascontiguousarray(L0_attn_Wk, np.float32),
        "wv": np.ascontiguousarray(L0_attn_Wv, np.float32),
        "wo": np.ascontiguousarray(L0_attn_Wo, np.float32),
        "rw": np.ascontiguousarray(L0_ffn_router_w, np.float32),
        "rb": np.ascontiguousarray(L0_ffn_router_bias, np.float32),
        "upw": np.ascontiguousarray(L0_ffn_up_proj, np.float32),
        "gw": np.ascontiguousarray(L0_ffn_gate_proj, np.float32),
        "dw": np.ascontiguousarray(L0_ffn_down_proj, np.float32),
        "shgT": np.ascontiguousarray(np.asarray(L0_ffn_sh_gate, np.float32).T),
        "shuT": np.ascontiguousarray(np.asarray(L0_ffn_sh_up, np.float32).T),
        "shdT": np.ascontiguousarray(np.asarray(L0_ffn_sh_down, np.float32).T),
        "ident": consts["ident"],
        "sel": consts["sel"],
        "bmask": consts["band"],
        "mT1": consts["mT1"],
        "mT0": consts["mT0"],
    }
    in_maps = []
    for c in range(8):
        b, half = c // 2, c % 2
        rows = x[b, half * R_CORE:(half + 1) * R_CORE, :]
        if half == 0:
            halo = np.zeros((8, D), np.float32)
            fmask = consts["fmask_even"]
            mT0f = consts["mT0z"]
        else:
            halo = x[b, half * R_CORE - 8: half * R_CORE, :]
            fmask = consts["band"]
            mT0f = consts["mT0"]
        m = dict(common)
        m["x"] = np.ascontiguousarray(np.concatenate([halo, rows], axis=0))
        m["fmask"] = fmask
        m["mT0f"] = mT0f
        in_maps.append(m)

    nc = build_program()
    res = run_bass_kernel_spmd(nc, in_maps, core_ids=list(range(8)))
    shards = [res.results[c]["out"] for c in range(8)]
    full = np.concatenate(shards, axis=0).reshape(Bsz, S, D)
    return full.astype(np.float32)


if __name__ == "__main__":
    nc = build_program()
    print("built ok")
